# version 06
# baseline (speedup 1.0000x reference)
import contextlib

import numpy as np
import ml_dtypes

import concourse.bass as bass
import concourse.mybir as mybir
from concourse.bass_utils import run_bass_kernel_spmd

F32 = mybir.dt.float32
BF16 = mybir.dt.bfloat16
I32 = mybir.dt.int32
I16 = mybir.dt.int16
F16 = mybir.dt.float16
AF = mybir.ActivationFunctionType
ALU = mybir.AluOpType
AX = mybir.AxisListType
BF = ml_dtypes.bfloat16

NCORES = 8
SEQS = 2
N = 2048
D = 2048
H = 16
HD = 128
C = 512
NCH = SEQS * N // C
TAU = C // 128
DC = D // 128
FF = 5632
FO = 2 * FF
GUT = FO // 128
FFC = FF // 128
DNT = D // 128
NSLAB = GUT + DNT


class Em:
    ENGS = ("sync", "tensor", "vector", "scalar", "gpsimd")

    def __init__(self):
        self.q = {e: [] for e in self.ENGS}
        self.sem = {}
        self.cnt = {}

    def add_sem(self, name, handle):
        self.sem[name] = handle
        self.cnt[name] = 0

    def wait(self, eng, name, n=None):
        if n is None:
            n = self.cnt[name]
        if n <= 0:
            return
        h = self.sem[name]
        self.q[eng].append(lambda e, h=h, n=n: e.wait_ge(h, n))

    OWN = {"vector": "V", "scalar": "S"}

    def op(self, eng, fn, inc=None, inck=1, dma=False):
        own = None if dma else self.OWN.get(eng)
        if own is not None:
            prev = self.cnt[own]
            if prev > 0:
                h0 = self.sem[own]
                self.q[eng].append(lambda e, h=h0, n=prev: e.wait_ge(h, n))
            inc, inck = own, 1
        if inc is not None:
            h = self.sem[inc]
            self.cnt[inc] += inck

            def run(e, fn=fn, h=h, k=inck):
                fn(e).then_inc(h, k)
        else:
            def run(e, fn=fn):
                fn(e)
        self.q[eng].append(run)


def build_nc(n_chunks=NCH, dbg=0):
    nc = bass.Bass("TRN2", target_bir_lowering=False)
    dt = nc.dram_tensor
    x_in = dt("x_in", [SEQS, N // C, DC, 128, C], F32, kind="ExternalInput").ap()
    x_out = dt("x_out", [SEQS, N // C, DC, 128, C], F32, kind="ExternalOutput").ap()
    wq_f = dt("wq_f", [DC, DC, 128, 128], F32, kind="ExternalInput").ap()
    wk_f = dt("wk_f", [DC, DC, 128, 128], F32, kind="ExternalInput").ap()
    wv_f = dt("wv_f", [DC, DC, 128, 128], F32, kind="ExternalInput").ap()
    wo_f = dt("wo_f", [DC, DC, 128, 128], F32, kind="ExternalInput").ap()
    wbeta_f = dt("wbeta_f", [128, DC, H], F32, kind="ExternalInput").ap()
    wg1_f = dt("wg1_f", [DC, 128, 128], F32, kind="ExternalInput").ap()
    wg2_f = dt("wg2_f", [128, D], F32, kind="ExternalInput").ap()
    gnorm_f = dt("gnorm_f", [128, DC], F32, kind="ExternalInput").ap()
    normw_f = dt("normw_f", [128, DC], F32, kind="ExternalInput").ap()
    wgu_f = dt("wgu_f", [GUT, DC, 128, 128], F32, kind="ExternalInput").ap()
    wdn_f = dt("wdn_f", [DNT, FFC, 128, 128], F32, kind="ExternalInput").ap()
    c_id = dt("c_id", [128, 128], BF16, kind="ExternalInput").ap()
    c_nid = dt("c_nid", [128, 128], BF16, kind="ExternalInput").ap()
    c_mu = dt("c_mu", [128, 128], BF16, kind="ExternalInput").ap()
    c_mud = dt("c_mud", [128, 128], BF16, kind="ExternalInput").ap()
    c_ones1 = dt("c_ones1", [1, 128], BF16, kind="ExternalInput").ap()
    c_onescol = dt("c_onescol", [128, 1], BF16, kind="ExternalInput").ap()
    c_idf = dt("c_idf", [128, 128], F32, kind="ExternalInput").ap()
    wq_s = dt("wq_s", [DC, 128, DC, 128], BF16, kind="Internal").ap()
    wk_s = dt("wk_s", [DC, 128, DC, 128], BF16, kind="Internal").ap()
    wv_s = dt("wv_s", [DC, 128, DC, 128], BF16, kind="Internal").ap()
    wo_s = dt("wo_s", [DC, 128, DC, 128], BF16, kind="Internal").ap()
    wgu_s = dt("wgu_s", [GUT, 128, DC, 128], BF16, kind="Internal").ap()
    wdn_s = dt("wdn_s", [DNT, 128, FFC, 128], BF16, kind="Internal").ap()

    ctx = contextlib.ExitStack()
    sb = lambda name, shape, dtp: ctx.enter_context(nc.sbuf_tensor(name, shape, dtp))
    ps = lambda name, shape, dtp: ctx.enter_context(nc.psum_tensor(name, shape, dtp))

    sx = sb("sx", [128, DC * C], F32)
    sA = sb("sA", [128, DC * C], BF16)
    sB = sb("sB", [128, FFC * C], BF16)
    sQKV = sb("sQKV", [128, 3 * C], BF16)
    sO = sb("sO", [128, DC * C], BF16)
    sS = sb("sS", [128, H * HD], F32)
    sSb = sb("sSb", [128, H * HD], BF16)
    sW1 = sb("sW1", [128, 2 * D], BF16)
    sWF = sb("sWF", [128, 12288], BF16)
    sWg1 = sb("sWg1", [128, D], BF16)
    sWg2 = sb("sWg2", [128, D], BF16)
    sWbeta = sb("sWbeta", [128, DC * H], BF16)
    sgnn = sb("sgnn", [128, DC], F32)
    snw = sb("snw", [128, DC], F32)
    scol = sb("scol", [128, GUT + DNT], F32)
    sbc = sb("sbc", [128, 2 * C], F32)
    sbb = sb("sbb", [16, C], BF16)
    sbT = sb("sbT", [128, (C // 64) * H], F32)
    sTN = sb("sTN", [128, 2 * 64], F32)
    sTP = sb("sTP", [128, 2 * 64], F32)
    sEf = sb("sEf", [128, 128], F32)
    sD = sb("sD", [128, 128], BF16)
    sHm = sb("sHm", [128, 64], BF16)
    sKt = sb("sKt", [128, 128], BF16)
    sy0 = sb("sy0", [128, 2 * C], F32)
    sz = sb("sz", [128, 2 * C], I16)
    ssm = sb("ssm", [128, 2 * C], F32)
    smxb = sb("smxb", [128, TAU], BF16)
    srg = sb("srg", [128, 3 * C], BF16)
    sg1 = sb("sg1", [128, C], BF16)
    ssg = sb("ssg", [128, C], BF16)
    srow = sb("srow", [1, 9 * C], F32)
    srw2 = sb("srw2", [1, 4], F32)
    sacc = sb("sacc", [128, 12], F32)
    saccb = sb("saccb", [128, 6], BF16)
    sck = sb("sck", [128, 8], F32)
    sthr = sb("sthr", [128, 4], F32)
    sid = sb("sid", [128, 128], BF16)
    sidf = sb("sidf", [128, 128], F32)
    snid = sb("snid", [128, 128], BF16)
    smu = sb("smu", [128, 128], BF16)
    smud = sb("smud", [128, 128], BF16)
    sof = sb("sof", [1, 128], F32)
    soc = sb("soc", [128, 1], BF16)

    pA = ps("pA", [128, 512], F32)
    pB = ps("pB", [128, 512], F32)
    pS1 = ps("pS1", [128, 512], F32)
    pS2 = ps("pS2", [128, 512], F32)
    pBC = ps("pBC", [128, 512], F32)
    pE = ps("pE", [128, 512], F32)
    pGH = ps("pGH", [128, 1024], BF16)
    pX = ps("pX", [128, 512], F32)

    em = Em()
    for s in ("T", "V", "S", "OD", "XL", "CO", "CR", "MW0", "MW1",
              "W1_0", "W1_1", "WF0", "WF1", "WF2", "WF3",
              "W1_0G", "W1_1G", "WF0G", "WF1G", "WF2G", "WF3G"):
        em.add_sem(s, ctx.enter_context(nc.semaphore("sem" + s)))

    (R_MEAN, R_VAR, R_T0, R_T1, R_RHO, R_MX, R_C, R_C3, R_S2) = range(9)
    R_ALPHA = R_MX
    R_EX2 = R_T1
    R_RSTD = R_S2
    R_OS2 = R_T1

    def row(i):
        return srow[:, i * C:(i + 1) * C]

    def mm(out, lhsT, rhs, start, stop, inc=None, f32=False):
        if f32 and inc:
            em.op("tensor", lambda e, o=out, l=lhsT, r=rhs, st=start, sp=stop:
                  e.matmul(o, l, r, start=st, stop=sp))
            em.op("tensor", lambda e: e.matmul(pX[0:1, 0:1], soc[:], soc[:],
                                               start=True, stop=True), inc=inc)
            return
        em.op("tensor", lambda e, o=out, l=lhsT, r=rhs, st=start, sp=stop:
              e.matmul(o, l, r, start=st, stop=sp), inc=inc)

    def barrier():
        mm(pX[0:1, 0:1], soc[:], soc[:], True, True, inc="T")
        for e, s in (("vector", "V"), ("scalar", "S")):
            em.op(e, lambda x: x.nop(), inc=s)
        snap = {s: em.cnt[s] for s in ("T", "V", "S")}
        for e in ("tensor", "vector", "scalar", "sync", "gpsimd"):
            for s, n in snap.items():
                em.wait(e, s, n)

    for i, val in [(0, 128.0), (1, 1e-6), (2, 1e-5)]:
        em.op("vector", lambda e, i=i, v=val: e.memset(sck[:, i:i + 1], v))
    em.op("vector", lambda e: e.memset(sS[:], 0.0))
    em.op("vector", lambda e: e.memset(sSb[:], 0.0))
    em.op("vector", lambda e: e.memset(sacc[:], 0.0))
    em.op("vector", lambda e: e.memset(sof[:], 1.0), inc="V")
    em.op("sync", lambda e: e.dma_start(out=sid[:], in_=c_id), inc="CO", inck=16)
    em.op("sync", lambda e: e.dma_start(out=sidf[:], in_=c_idf), inc="CO", inck=16)
    em.op("sync", lambda e: e.dma_start(out=snid[:], in_=c_nid), inc="CO", inck=16)
    em.op("sync", lambda e: e.dma_start(out=smu[:], in_=c_mu), inc="CO", inck=16)
    em.op("sync", lambda e: e.dma_start(out=smud[:], in_=c_mud), inc="CO", inck=16)
    em.op("sync", lambda e: e.dma_start(out=soc[:], in_=c_onescol), inc="CO", inck=16)
    em.op("sync", lambda e: e.dma_start(out=sgnn[:], in_=gnorm_f), inc="CO", inck=16)
    em.op("sync", lambda e: e.dma_start(out=snw[:], in_=normw_f), inc="CO", inck=16)
    em.op("gpsimd", lambda e: e.dma_start(
        out=sWg1[:].rearrange("p (a o) -> p a o", a=DC),
        in_=wg1_f.rearrange("a p o -> p a o")), inc="CR", inck=16)
    em.op("gpsimd", lambda e: e.dma_start(out=sWg2[:], in_=wg2_f), inc="CR", inck=16)
    em.op("gpsimd", lambda e: e.dma_start(
        out=sWbeta[:], in_=wbeta_f.rearrange("p a h -> p (a h)")), inc="CR", inck=16)
    em.wait("vector", "CO")
    em.op("vector", lambda e: e.tensor_scalar(sgnn[:], sgnn[:], -1.0, None, op0=ALU.mult), inc="V")
    v_static = em.cnt["V"]
    em.wait("tensor", "V", v_static)
    em.wait("scalar", "V", v_static)
    em.wait("tensor", "CO")
    em.wait("scalar", "CO")
    em.wait("tensor", "CR")

    mw_pieces = []
    for i in range(GUT):
        g3 = wgu_f[i].rearrange("a p o -> p a o")
        for k in range(4):
            mw_pieces.append((g3[:, 4 * k:4 * (k + 1), :], 512, 0))
    for ot in range(DNT):
        dn3 = wdn_f[ot].rearrange("a p o -> p a o")
        for k in range(11):
            mw_pieces.append((dn3[:, 4 * k:4 * (k + 1), :], 512, 1))
    NQ = len(mw_pieces)
    mw_state = {"i": 0, "s": [0] * NQ, "v": [0, 0]}

    def meanw_quanta(k):
        for _ in range(k):
            i = mw_state["i"]
            if i >= NQ:
                return
            slot = i % 2
            sem = "MW%d" % slot
            src, width, a = mw_pieces[i]
            dstp = sy0[:, slot * 512:slot * 512 + width]
            if i >= 2:
                em.wait("sync", "S", mw_state["s"][i - 2])
            em.op("sync", lambda e, d=dstp, s=src, w=width: e.dma_start(
                out=d.rearrange("p (a o) -> p a o", o=128), in_=s), inc=sem, inck=16, dma=True)
            em.wait("scalar", sem)
            if i > 0:
                em.wait("scalar", "V")
            em.op("scalar", lambda e, s=dstp: e.activation(
                s, s, AF.Abs, accum_out=sacc[:, 4:5]), inc="S")
            mw_state["s"][i] = em.cnt["S"]
            em.wait("vector", "S", mw_state["s"][i])
            em.op("vector", lambda e, a=a: e.tensor_tensor(
                sacc[:, 5:6], sacc[:, 4:5], sacc[:, 2 + a:3 + a], ALU.subtract))
            em.op("vector", lambda e, a=a: e.tensor_tensor(
                sacc[:, 6:7], sacc[:, a:a + 1], sacc[:, 5:6], ALU.add))
            em.op("vector", lambda e, a=a: e.tensor_tensor(
                sacc[:, 7:8], sacc[:, 6:7], sacc[:, a:a + 1], ALU.subtract))
            em.op("vector", lambda e, a=a: e.tensor_tensor(
                sacc[:, 2 + a:3 + a], sacc[:, 7:8], sacc[:, 5:6], ALU.subtract))
            em.op("vector", lambda e, a=a: e.tensor_copy(
                sacc[:, a:a + 1], sacc[:, 6:7]), inc="V")
            mw_state["i"] = i + 1

    w1_lastT = [0, 0]
    wf_lastT = [0, 0, 0, 0]

    def ternarize_region(src, dst, width, thp, thn):
        lo = 0
        while lo < width:
            hi = min(lo + 1024, width)
            em.op("vector", lambda e, lo=lo, hi=hi, thn=thn, src=src: e.tensor_scalar(
                sy0[:, 0:hi - lo], src[:, lo:hi], thn, None, op0=ALU.is_lt))
            em.op("vector", lambda e, lo=lo, hi=hi, thp=thp, src=src, dst=dst: e.scalar_tensor_tensor(
                dst[:, lo:hi], src[:, lo:hi], thp, sy0[:, 0:hi - lo],
                op0=ALU.is_gt, op1=ALU.subtract), inc="V" if hi == width else None)
            lo = hi

    for ch in range(n_chunks):
        first = ch == 0
        seq, chs = divmod(ch, N // C)
        if ch % (N // C) == 0 and ch > 0:
            em.op("vector", lambda e: e.memset(sS[:], 0.0))
            em.op("vector", lambda e: e.memset(sSb[:], 0.0), inc="V")
            em.wait("tensor", "V")

        if ch > 0:
            em.wait("sync", "OD")
        em.op("sync", lambda e, s=seq, c=chs: e.dma_start(
            out=sx[:].rearrange("p (a t) -> p a t", a=DC),
            in_=x_in[s, c].rearrange("a p t -> p a t")), inc="XL", inck=16)
        xl_x = em.cnt["XL"]

        if dbg == 3:
            barrier()
            em.wait("sync", "V")
            em.op("sync", lambda e, s=seq, c=chs: e.dma_start(
                out=x_out[s, c].rearrange("a p t -> p a t"),
                in_=sx[:].rearrange("p (a t) -> p a t", a=DC)), inc="OD", inck=16, dma=True)
            continue
        def ln_stats():
            em.wait("scalar", "XL", xl_x)
            tprev = [0, 0]
            bufs = [(srg[:, 0:C], srg[:, C:2 * C]), (srg[:, 2 * C:3 * C], ssg[:, 0:C])]
            for g in range(DC):
                b = g % 2
                xb, xs = bufs[b]
                if g >= 2:
                    em.wait("scalar", "T", tprev[b])
                em.op("scalar", lambda e, o=xb, g=g: e.activation(
                    o, sx[:, g * C:(g + 1) * C], AF.Identity), inc="S")
                em.wait("tensor", "S")
                mm(pS1[0:1, 0:C], soc[:], xb, g == 0, g == DC - 1)
                em.op("scalar", lambda e, o=xs, i=xb: e.activation(o, i, AF.Square), inc="S")
                em.wait("tensor", "S")
                mm(pS2[0:1, 0:C], soc[:], xs, g == 0, g == DC - 1, inc="T")
                tprev[b] = em.cnt["T"]
            em.wait("vector", "T")
            em.op("vector", lambda e: e.tensor_scalar(row(R_MEAN), pS1[0:1, 0:C], 1.0 / D, None, op0=ALU.mult))
            em.op("vector", lambda e: e.tensor_scalar(row(R_EX2), pS2[0:1, 0:C], 1.0 / D, None, op0=ALU.mult))
            em.op("vector", lambda e: e.tensor_tensor(row(R_VAR), row(R_MEAN), row(R_MEAN), ALU.mult))
            em.op("vector", lambda e: e.tensor_tensor(row(R_VAR), row(R_EX2), row(R_VAR), ALU.subtract), inc="V")

        ln_stats()
        em.wait("scalar", "V")
        em.op("scalar", lambda e: e.activation(row(R_RSTD), row(R_VAR), AF.Sqrt, bias=sck[0:1, 2:3]), inc="S")
        em.wait("vector", "S")
        em.op("vector", lambda e: e.reciprocal(row(R_RSTD), row(R_RSTD)))
        em.op("vector", lambda e: e.tensor_tensor(row(R_T0), row(R_MEAN), row(R_RSTD), ALU.mult), inc="V")
        em.wait("tensor", "V")
        mm(pBC[:, 0:C], sof[:], row(R_RSTD), True, True, f32=True)
        mm(pS1[:, 0:C], sof[:], row(R_T0), True, True, inc="T", f32=True)
        em.wait("vector", "T")
        em.op("vector", lambda e: e.tensor_copy(sbc[:, 0:C], pBC[:, 0:C]))
        em.op("vector", lambda e: e.tensor_copy(sbc[:, C:2 * C], pS1[:, 0:C]), inc="V")
        for g in range(DC):
            em.op("vector", lambda e, g=g: e.tensor_tensor(
                sA[:, g * C:(g + 1) * C], sx[:, g * C:(g + 1) * C], sbc[:, 0:C], ALU.mult))
            em.op("vector", lambda e, g=g: e.tensor_tensor(
                sA[:, g * C:(g + 1) * C], sA[:, g * C:(g + 1) * C], sbc[:, C:2 * C], ALU.subtract),
                inc="V" if g == DC - 1 else None)
        barrier()
        if dbg == 2:
            em.wait("sync", "V")
            em.op("sync", lambda e, s=seq, c=chs: e.dma_start(
                out=x_out[s, c].rearrange("a p t -> p a t"),
                in_=sx[:].rearrange("p (a t) -> p a t", a=DC)), inc="OD", inck=16, dma=True)
            continue

        for dc in range(DC):
            mm(pS2[0:16, 0:C], sWbeta[:, dc * H:(dc + 1) * H], sA[:, dc * C:(dc + 1) * C],
               dc == 0, dc == DC - 1, inc="T" if dc == DC - 1 else None)
        em.wait("scalar", "T")
        em.op("scalar", lambda e: e.activation(sbb[:], pS2[0:16, 0:C], AF.Sigmoid), inc="S")
        em.wait("vector", "S")
        em.op("vector", lambda e: e.tensor_scalar(sbb[:], sbb[:], 2.0, None, op0=ALU.mult), inc="V")
        em.wait("tensor", "V")
        NB64 = C // 64
        for t in range(NB64):
            em.op("tensor", lambda e, t=t: e.transpose(
                pGH[0:64, t * H:(t + 1) * H], sbb[:, t * 64:(t + 1) * 64], sid[0:16, 0:16]),
                inc="T" if t == NB64 - 1 else None)
        em.wait("vector", "T")
        em.op("vector", lambda e: e.tensor_scalar(
            sbT[0:64, 0:NB64 * H], pGH[0:64, 0:NB64 * H], -1.0, None, op0=ALU.mult), inc="V")
        v_beta = em.cnt["V"]
        for dc in range(DC):
            mm(pA[:, 0:C], sWg1[:, dc * 128:(dc + 1) * 128], sA[:, dc * C:(dc + 1) * C],
               dc == 0, dc == DC - 1, inc="T" if dc == DC - 1 else None)
        em.wait("scalar", "T")
        em.op("scalar", lambda e: e.activation(sg1[:], pA[:, 0:C], AF.Identity), inc="S")
        s_g1 = em.cnt["S"]
        em.wait("tensor", "S", s_g1)
        if dbg == 5:
            barrier()
            em.wait("sync", "V")
            em.op("sync", lambda e, s=seq, c=chs: e.dma_start(
                out=x_out[s, c].rearrange("a p t -> p a t"),
                in_=sx[:].rearrange("p (a t) -> p a t", a=DC)), inc="OD", inck=16, dma=True)
            continue

        for h in range(1 if dbg in (1,2,3,4,5,6,7,8,9,10,11,12,13,14,26,27,28,29,30,31,32,33,35,40,41,42) else H):
            if first and dbg != 4:
                meanw_quanta(36 if h < 15 else 10 ** 6)

            def wslice(idx, wsrc_s, wsrc_f, ot):
                buf = idx % 2
                sem = "W1_%d" % buf
                dst = sW1[:, buf * D:(buf + 1) * D]
                if first:
                    em.wait("gpsimd", "T", w1_lastT[buf])
                    em.wait("gpsimd", sem)
                    em.op("gpsimd", lambda e, d=dst, s=wsrc_f[ot]: e.dma_start(
                        out=d.rearrange("p (a o) -> p a o", a=DC),
                        in_=s.rearrange("a p o -> p a o")), inc=sem + "G", inck=16)
                    ldc = em.cnt[sem + "G"]
                    em.wait("sync", sem + "G", ldc)
                    em.op("sync", lambda e, d=wsrc_s[ot], s=dst: e.dma_start(
                        out=d.rearrange("p a o -> p (a o)"), in_=s), inc=sem, inck=16)
                    em.wait("tensor", sem + "G", ldc)
                else:
                    em.wait("sync", "T", w1_lastT[buf])
                    em.wait("sync", sem)
                    em.op("sync", lambda e, d=dst, s=wsrc_s[ot]: e.dma_start(
                        out=d, in_=s.rearrange("p a o -> p (a o)")), inc=sem, inck=16)
                    em.wait("tensor", sem)
                return dst

            qh, kh, vh = sQKV[:, 0:C], sQKV[:, C:2 * C], sQKV[:, 2 * C:3 * C]
            wqd = wslice(3 * h, wq_s, wq_f, h)
            for dc in range(DC):
                mm(pA[:, 0:C], wqd[:, dc * 128:(dc + 1) * 128], sA[:, dc * C:(dc + 1) * C],
                   dc == 0, dc == DC - 1, inc="T" if dc == DC - 1 else None)
            w1_lastT[3 * h % 2] = em.cnt["T"]
            em.wait("scalar", "T")
            em.op("scalar", lambda e, o=qh: e.activation(o, pA[:, 0:C], AF.Silu), inc="S")
            s_q = em.cnt["S"]
            wkd = wslice(3 * h + 1, wk_s, wk_f, h)
            for dc in range(DC):
                mm(pB[:, 0:C], wkd[:, dc * 128:(dc + 1) * 128], sA[:, dc * C:(dc + 1) * C],
                   dc == 0, dc == DC - 1, inc="T" if dc == DC - 1 else None)
            w1_lastT[(3 * h + 1) % 2] = em.cnt["T"]
            em.wait("scalar", "T")
            em.op("scalar", lambda e, o=kh: e.activation(o, pB[:, 0:C], AF.Silu), inc="S")
            wvd = wslice(3 * h + 2, wv_s, wv_f, h)
            em.wait("tensor", "S", s_q)
            for dc in range(DC):
                mm(pA[:, 0:C], wvd[:, dc * 128:(dc + 1) * 128], sA[:, dc * C:(dc + 1) * C],
                   dc == 0, dc == DC - 1, inc="T" if dc == DC - 1 else None)
            w1_lastT[(3 * h + 2) % 2] = em.cnt["T"]
            em.wait("scalar", "T")
            em.op("scalar", lambda e, o=vh: e.activation(o, pA[:, 0:C], AF.Identity, scale=-1.0), inc="S")
            s_proj = em.cnt["S"]
            if dbg == 7:
                continue
            em.wait("vector", "S", s_proj - 1)
            em.op("vector", lambda e, k=kh: e.tensor_tensor(srg[:, 0:C], k, k, ALU.mult), inc="V")
            if dbg == 8:
                continue
            em.wait("tensor", "V")
            mm(pS1[0:1, 0:C], soc[:], srg[:, 0:C], True, True, inc="T")
            em.wait("scalar", "T")
            em.op("scalar", lambda e: e.activation(row(R_T0), pS1[0:1, 0:C], AF.Sqrt, bias=sck[0:1, 1:2]), inc="S")
            if dbg == 9:
                continue
            em.wait("vector", "S")
            em.op("vector", lambda e: e.reciprocal(row(R_T0), row(R_T0)), inc="V")
            if dbg == 10:
                continue
            em.wait("tensor", "V")
            mm(pBC[:, 0:C], sof[:], row(R_T0), True, True, inc="T", f32=True)
            if dbg == 11:
                continue
            em.wait("vector", "T")
            if dbg == 12:
                em.op("vector", lambda e, k=kh: e.tensor_tensor(srg[:, 0:C], k, pBC[:, 0:C], ALU.mult), inc="V")
                continue
            if dbg == 13:
                em.op("scalar", lambda e: e.activation(srg[:, C:2 * C].bitcast(F32) if False else srg[:, 0:C], pBC[:, 0:C], AF.Identity), inc="S")
                em.wait("vector", "S")
                em.op("vector", lambda e, k=kh: e.tensor_tensor(k, k, srg[:, 0:C], ALU.mult), inc="V")
                continue
            em.op("vector", lambda e, k=kh: e.tensor_tensor(k, k, pBC[:, 0:C], ALU.mult), inc="V")
            em.wait("tensor", "V")
            em.wait("tensor", "S", s_proj)
            em.wait("tensor", "V", v_beta)
            if dbg == 6:
                continue

            NLV = 6
            for blk in range(C // 64):
                bsl = slice(blk * 64, (blk + 1) * 64)
                Sh = sS[:, h * HD:(h + 1) * HD]
                Snb = sSb[:, h * HD:(h + 1) * HD]
                bn = sbT[0:64, blk * H + h:blk * H + h + 1]
                mm(pS2[0:64, 0:64], kh[:, bsl], kh[:, bsl], True, True)
                if dbg == 26:
                    em.op("tensor", lambda e: e.matmul(pX[0:1, 0:1], soc[:], soc[:], start=True, stop=True), inc="T")
                    break
                mm(pS1[0:64, 0:64], kh[:, bsl], qh[:, bsl], True, True)
                mm(pE[0:64, 128:256], vh[:, bsl], snid[:], True, False)
                mm(pE[0:64, 128:256], kh[:, bsl], Snb, False, True, inc="T")
                if dbg == 27:
                    break
                em.wait("scalar", "T")
                em.op("scalar", lambda e: e.activation(sEf[0:64, :], pE[0:64, 128:256], AF.Identity), inc="S")
                if dbg == 28:
                    break
                em.wait("vector", "T")
                if dbg == 29:
                    em.op("vector", lambda e, bn=bn: e.scalar_tensor_tensor(
                        sTN[0:64, 0:64], pS2[0:64, 0:64], bn, smu[0:64, 0:64], op0=ALU.mult, op1=ALU.mult), inc="V")
                    break
                if dbg == 40:
                    em.op("vector", lambda e: e.scalar_tensor_tensor(
                        sTN[0:64, 0:64], pS2[0:64, 0:64], -1.0, smu[0:64, 0:64], op0=ALU.mult, op1=ALU.mult), inc="V")
                    break
                if dbg == 41:
                    em.op("vector", lambda e: e.tensor_tensor(
                        sTN[0:64, 0:64], pS2[0:64, 0:64], smu[0:64, 0:64], ALU.mult), inc="V")
                    break
                if dbg == 42:
                    em.op("vector", lambda e: e.tensor_copy(sTN[0:64, 0:64], pS2[0:64, 0:64]), inc="V")
                    break
                if dbg == 35:
                    em.op("vector", lambda e: e.tensor_tensor(
                        sHm[0:64, 0:64], pS1[0:64, 0:64], smud[0:64, 0:64], ALU.mult), inc="V")
                    break
                em.op("vector", lambda e, bn=bn: e.scalar_tensor_tensor(
                    sTN[0:64, 0:64], pS2[0:64, 0:64], bn, smu[0:64, 0:64], op0=ALU.mult, op1=ALU.mult))
                em.op("vector", lambda e: e.tensor_tensor(
                    sHm[0:64, 0:64], pS1[0:64, 0:64], smud[0:64, 0:64], ALU.mult), inc="V")
                if dbg == 30:
                    break
                em.wait("tensor", "V")
                em.op("tensor", lambda e: e.transpose(pS1[0:64, 0:64], sTN[0:64, 0:64], sidf[0:64, 0:64]))
                em.op("tensor", lambda e: e.matmul(pX[0:1, 0:1], soc[:], soc[:],
                                                   start=True, stop=True), inc="T")
                em.wait("scalar", "T")
                em.op("scalar", lambda e: e.activation(sTP[0:64, 0:64], pS1[0:64, 0:64], AF.Identity), inc="S")
                if dbg == 31:
                    break
                em.wait("tensor", "S")
                for lv in range(NLV):
                    pp = lv % 2
                    npp = 1 - pp
                    tn = sTN[0:64, pp * 64:(pp + 1) * 64]
                    tp = sTP[0:64, pp * 64:(pp + 1) * 64]
                    tnn = sTN[0:64, npp * 64:(npp + 1) * 64]
                    tpn = sTP[0:64, npp * 64:(npp + 1) * 64]
                    mm(pE[0:64, 128:256], tn, sEf[0:64, :], True, True,
                       inc=None if lv < NLV - 1 else "T", f32=True)
                    if lv < NLV - 1:
                        mm(pS1[0:64, 0:64], tn, tp, True, True, f32=True)
                        mm(pS2[0:64, 0:64], tp, tn, True, True, inc="T", f32=True)
                    em.wait("scalar", "T")
                    em.wait("vector", "T")
                    if lv < NLV - 1:
                        em.op("scalar", lambda e, d=tpn: e.activation(
                            d, pS1[0:64, 0:64], AF.Identity))
                        em.op("scalar", lambda e, d=tnn: e.activation(
                            d, pS2[0:64, 0:64], AF.Identity), inc="S")
                        em.op("vector", lambda e: e.tensor_tensor(
                            sEf[0:64, :], sEf[0:64, :], pE[0:64, 128:256], ALU.add), inc="V")
                        em.wait("tensor", "S")
                        em.wait("tensor", "V")
                    else:
                        em.op("vector", lambda e: e.tensor_tensor(
                            sEf[0:64, :], sEf[0:64, :], pE[0:64, 128:256], ALU.add))
                        em.op("vector", lambda e, bn=bn: e.tensor_scalar(
                            sD[0:64, :], sEf[0:64, :], bn, None, op0=ALU.mult), inc="V")
                if dbg == 32:
                    break
                em.wait("tensor", "V")
                mm(pA[:, 0:64], Snb, qh[:, bsl], True, False)
                mm(pA[:, 0:64], sD[0:64, :], sHm[0:64, 0:64], False, True)
                mm(pE[0:64, 256:384], kh[:, bsl], sid[:], True, True, inc="T")
                em.wait("vector", "T")
                em.op("vector", lambda e: e.tensor_copy(sKt[0:64, :], pE[0:64, 256:384]))
                em.op("vector", lambda e, h=h, blk=blk: e.tensor_copy(
                    sO[:, h * C + blk * 64:h * C + (blk + 1) * 64], pA[:, 0:64]), inc="V")
                if dbg == 33:
                    break
                em.wait("tensor", "V")
                mm(pA[:, 128:256], sKt[0:64, :], sD[0:64, :], True, True, inc="T")
                em.wait("vector", "T")
                em.op("vector", lambda e, S=Sh: e.tensor_tensor(S, S, pA[:, 128:256], ALU.subtract))
                em.op("vector", lambda e, S=Sh, Sn=Snb: e.tensor_scalar(
                    Sn, S, -1.0, None, op0=ALU.mult), inc="V")
                em.wait("tensor", "V")
                if 1 <= dbg <= 13:
                    break

            if dbg in (1,2,3,4,5,6,7,8,9,10,11,12,13,15,26,27,28,29,30,31,32,33,35,40,41,42):
                continue
            mm(pB[:, 0:C], sWg2[:, h * 128:(h + 1) * 128], sg1[:], True, True, inc="T")
            em.wait("scalar", "T")
            em.op("scalar", lambda e: e.activation(ssg[:], pB[:, 0:C], AF.Sigmoid), inc="S")
            em.wait("vector", "S")
            oh = sO[:, h * C:(h + 1) * C]
            em.op("vector", lambda e, o=oh: e.tensor_tensor(o, o, ssg[:], ALU.mult))
            em.op("vector", lambda e, o=oh: e.tensor_tensor(srg[:, 0:C], o, o, ALU.mult), inc="V")
            em.wait("tensor", "V")
            mm(pS1[0:1, 0:C], soc[:], srg[:, 0:C], True, True, inc="T")
            em.wait("scalar", "T")
            em.op("scalar", lambda e: e.activation(
                row(R_T0), pS1[0:1, 0:C], AF.Sqrt, bias=sck[0:1, 1:2], scale=1.0 / HD), inc="S")
            em.wait("vector", "S")
            em.op("vector", lambda e: e.reciprocal(row(R_T0), row(R_T0)), inc="V")
            em.wait("tensor", "V")
            mm(pBC[:, 0:C], sof[:], row(R_T0), True, True, inc="T", f32=True)
            em.wait("vector", "T")
            em.op("vector", lambda e, o=oh, h=h: e.scalar_tensor_tensor(
                o, o, sgnn[:, h:h + 1], pBC[:, 0:C], op0=ALU.mult, op1=ALU.mult), inc="V")
            em.wait("tensor", "V")

        if dbg in (14, 15, 20, 26, 27, 28, 29, 30, 31, 32, 33, 35,40,41,42):
            em.wait("sync", "V")
            em.op("sync", lambda e, s=seq, c=chs: e.dma_start(
                out=x_out[s, c].rearrange("a p t -> p a t"),
                in_=sx[:].rearrange("p (a t) -> p a t", a=DC)), inc="OD", inck=16, dma=True)
            continue
        for ot in range(0 if dbg else DC):
            buf = ot % 2
            dst = sW1[:, buf * D:(buf + 1) * D]
            sem = "W1_%d" % buf
            if first:
                em.wait("gpsimd", "T", w1_lastT[buf])
                em.wait("gpsimd", sem)
                em.op("gpsimd", lambda e, d=dst, s=wo_f[ot]: e.dma_start(
                    out=d.rearrange("p (a o) -> p a o", a=DC),
                    in_=s.rearrange("a p o -> p a o")), inc=sem + "G", inck=16)
                ldc = em.cnt[sem + "G"]
                em.wait("sync", sem + "G", ldc)
                em.op("sync", lambda e, d=wo_s[ot], s=dst: e.dma_start(
                    out=d.rearrange("p a o -> p (a o)"), in_=s), inc=sem, inck=16)
                em.wait("tensor", sem + "G", ldc)
            else:
                em.wait("sync", "T", w1_lastT[buf])
                em.wait("sync", sem)
                em.op("sync", lambda e, d=dst, s=wo_s[ot]: e.dma_start(
                    out=d, in_=s.rearrange("p a o -> p (a o)")), inc=sem, inck=16)
                em.wait("tensor", sem)
            pb = pA if ot % 2 == 0 else pB
            em.wait("tensor", "V")
            for dc in range(DC):
                mm(pb[:, 0:C], dst[:, dc * 128:(dc + 1) * 128], sO[:, dc * C:(dc + 1) * C],
                   dc == 0, dc == DC - 1, inc="T" if dc == DC - 1 else None)
            w1_lastT[buf] = em.cnt["T"]
            em.wait("vector", "T")
            em.op("vector", lambda e, ot=ot, pb=pb: e.tensor_tensor(
                sx[:, ot * C:(ot + 1) * C], sx[:, ot * C:(ot + 1) * C], pb[:, 0:C], ALU.add), inc="V")
        barrier()
        if dbg == 21:
            em.wait("sync", "V")
            em.op("sync", lambda e, s=seq, c=chs: e.dma_start(
                out=x_out[s, c].rearrange("a p t -> p a t"),
                in_=sx[:].rearrange("p (a t) -> p a t", a=DC)), inc="OD", inck=16, dma=True)
            continue

        if first and not dbg:
            for a in range(2):
                em.op("vector", lambda e, a=a: e.tensor_copy(saccb[:, 3 * a:3 * a + 1], sacc[:, a:a + 1]))
                em.op("vector", lambda e, a=a: e.tensor_tensor(
                    sacc[:, 8 + a:9 + a], sacc[:, a:a + 1], saccb[:, 3 * a:3 * a + 1], ALU.subtract))
                em.op("vector", lambda e, a=a: e.tensor_copy(saccb[:, 3 * a + 1:3 * a + 2], sacc[:, 8 + a:9 + a]))
                em.op("vector", lambda e, a=a: e.tensor_tensor(
                    sacc[:, 8 + a:9 + a], sacc[:, 8 + a:9 + a], saccb[:, 3 * a + 1:3 * a + 2], ALU.subtract))
                em.op("vector", lambda e, a=a: e.tensor_copy(saccb[:, 3 * a + 2:3 * a + 3], sacc[:, 8 + a:9 + a]),
                      inc="V" if a == 1 else None)
            em.wait("tensor", "V")
            mm(pS1[0:1, 0:1], saccb[:, 0:1], soc[:], True, False)
            mm(pS1[0:1, 0:1], saccb[:, 1:2], soc[:], False, False)
            mm(pS1[0:1, 0:1], saccb[:, 2:3], soc[:], False, True)
            mm(pS2[0:1, 0:1], saccb[:, 3:4], soc[:], True, False)
            mm(pS2[0:1, 0:1], saccb[:, 4:5], soc[:], False, False)
            mm(pS2[0:1, 0:1], saccb[:, 5:6], soc[:], False, True, inc="T")
            em.wait("vector", "T")
            em.op("vector", lambda e: e.tensor_scalar(
                srw2[:, 0:1], pS1[0:1, 0:1], 1.0 / (FO * D), 1e-5, op0=ALU.mult, op1=ALU.max))
            em.op("vector", lambda e: e.tensor_scalar(
                srw2[:, 1:2], pS2[0:1, 0:1], 1.0 / (D * FF), 1e-5, op0=ALU.mult, op1=ALU.max))
            em.op("vector", lambda e: e.tensor_scalar(
                srw2[:, 2:3], srw2[:, 0:1], 0.5, None, op0=ALU.mult))
            em.op("vector", lambda e: e.tensor_scalar(
                srw2[:, 3:4], srw2[:, 1:2], 0.5, None, op0=ALU.mult), inc="V")
            em.wait("tensor", "V")
            mm(pBC[:, 0:1], sof[:], srw2[:, 2:3], True, True)
            mm(pS1[:, 0:1], sof[:], srw2[:, 3:4], True, True, inc="T", f32=True)
            em.wait("vector", "T")
            em.op("vector", lambda e: e.tensor_copy(sthr[:, 0:1], pBC[:, 0:1]))
            em.op("vector", lambda e: e.tensor_scalar(sthr[:, 1:2], pBC[:, 0:1], -1.0, None, op0=ALU.mult))
            em.op("vector", lambda e: e.tensor_copy(sthr[:, 2:3], pS1[:, 0:1]))
            em.op("vector", lambda e: e.tensor_scalar(sthr[:, 3:4], pS1[:, 0:1], -1.0, None, op0=ALU.mult), inc="V")
            barrier()

        if dbg:
            em.wait("sync", "V")
            em.op("sync", lambda e, s=seq, c=chs: e.dma_start(
                out=x_out[s, c].rearrange("a p t -> p a t"),
                in_=sx[:].rearrange("p (a t) -> p a t", a=DC)), inc="OD", inck=16, dma=True)
            continue
        ln_stats()
        em.op("vector", lambda e: e.tensor_scalar(row(R_T0), row(R_VAR), 1e-5, None, op0=ALU.add))
        em.op("vector", lambda e: e.reciprocal(row(R_T1), row(R_T0)))
        em.op("vector", lambda e: e.tensor_tensor(row(R_T1), row(R_VAR), row(R_T1), ALU.mult))
        em.op("vector", lambda e: e.tensor_scalar(row(R_T1), row(R_T1), 1e-5, None, op0=ALU.add))
        em.op("vector", lambda e: e.tensor_tensor(row(R_T1), row(R_T0), row(R_T1), ALU.mult), inc="V")
        em.wait("scalar", "V")
        em.op("scalar", lambda e: e.activation(row(R_T1), row(R_T1), AF.Sqrt), inc="S")
        em.wait("vector", "S")
        em.op("vector", lambda e: e.reciprocal(row(R_RHO), row(R_T1)), inc="V")
        em.wait("tensor", "V")
        mm(pBC[:, 0:C], sof[:], row(R_MEAN), True, True, inc="T", f32=True)
        em.wait("vector", "T")
        em.op("vector", lambda e: e.tensor_copy(sbc[:, 0:C], pBC[:, 0:C]), inc="V")
        sA16 = sA[:].bitcast(F16)
        for g in range(DC):
            em.op("vector", lambda e, g=g, sA16=sA16: e.tensor_tensor(
                sA16[:, g * C:(g + 1) * C], sx[:, g * C:(g + 1) * C], sbc[:, 0:C], ALU.subtract),
                inc="V" if g == DC - 1 else None)
        em.op("vector", lambda e: e.memset(ssm[:, 0:C], -3e38))
        em.op("vector", lambda e: e.memset(ssm[:, C:2 * C], 3e38), inc="V")
        for g in range(DC):
            em.op("vector", lambda e, g=g, sA16=sA16: e.scalar_tensor_tensor(
                ssm[:, 0:C], sA16[:, g * C:(g + 1) * C], snw[:, g:g + 1], ssm[:, 0:C],
                op0=ALU.mult, op1=ALU.max))
            em.op("vector", lambda e, g=g, sA16=sA16: e.scalar_tensor_tensor(
                ssm[:, C:2 * C], sA16[:, g * C:(g + 1) * C], snw[:, g:g + 1], ssm[:, C:2 * C],
                op0=ALU.mult, op1=ALU.min), inc="V" if g == DC - 1 else None)
        em.op("vector", lambda e: e.scalar_tensor_tensor(
            srg[:, 0:C], ssm[:, C:2 * C], -1.0, ssm[:, 0:C], op0=ALU.mult, op1=ALU.max), inc="V")

        def token_rows(src_bf, dst_row):
            em.wait("tensor", "V")
            for t in range(TAU):
                em.op("tensor", lambda e, t=t, s=src_bf: e.transpose(
                    pGH[:, t * 128:(t + 1) * 128], s[:, t * 128:(t + 1) * 128], sid[:]),
                    inc="T" if t == TAU - 1 else None)
            em.wait("vector", "T")
            for t in range(TAU):
                em.op("vector", lambda e, t=t: e.tensor_reduce(
                    sacc[:, 8 + t:8 + t + 1], pGH[:, t * 128:(t + 1) * 128],
                    axis=AX.X, op=ALU.max), inc="V" if t == TAU - 1 else None)
            em.op("vector", lambda e: e.tensor_copy(smxb[:], sacc[:, 8:8 + TAU]), inc="V")
            em.wait("tensor", "V")
            for t in range(TAU):
                em.op("tensor", lambda e, t=t: e.transpose(
                    pGH[0:1, 512 + t * 128:512 + (t + 1) * 128], smxb[:, t:t + 1], sid[:]),
                    inc="T" if t == TAU - 1 else None)
            em.wait("vector", "T")
            em.op("vector", lambda e, d=dst_row: e.tensor_copy(d, pGH[0:1, 512:512 + C]), inc="V")

        token_rows(srg[:, 0:C], row(R_MX))
        em.op("vector", lambda e: e.tensor_tensor(row(R_MX), row(R_MX), row(R_RHO), ALU.mult))
        em.op("vector", lambda e: e.tensor_scalar(row(R_C), row(R_MX), 1e-5, None, op0=ALU.max))
        em.op("vector", lambda e: e.reciprocal(row(R_T0), row(R_C)))
        em.op("vector", lambda e: e.tensor_scalar(row(R_T0), row(R_T0), 127.0, None, op0=ALU.mult))
        em.op("vector", lambda e: e.tensor_tensor(row(R_ALPHA), row(R_RHO), row(R_T0), ALU.mult))
        em.op("vector", lambda e: e.tensor_scalar(
            row(R_C), row(R_C), srw2[:, 0:1], 1.0 / 127.0, op0=ALU.mult, op1=ALU.mult))
        em.op("vector", lambda e: e.tensor_tensor(row(R_C3), row(R_C), row(R_C), ALU.mult))
        em.op("vector", lambda e: e.tensor_tensor(row(R_C3), row(R_C3), row(R_C), ALU.mult), inc="V")
        em.wait("tensor", "V")
        mm(pBC[:, 0:C], sof[:], row(R_ALPHA), True, True, f32=True)
        mm(pS1[:, 0:C], sof[:], row(R_C3), True, True, inc="T", f32=True)
        em.wait("vector", "T")
        em.op("vector", lambda e: e.tensor_copy(sbc[:, C:2 * C], pBC[:, 0:C]))
        em.op("vector", lambda e: e.tensor_copy(sbc[:, 0:C], pS1[:, 0:C]), inc="V")
        barrier()

        if dbg == 22:
            em.wait("sync", "V")
            em.op("sync", lambda e, s=seq, c=chs: e.dma_start(
                out=x_out[s, c].rearrange("a p t -> p a t"),
                in_=sx[:].rearrange("p (a t) -> p a t", a=DC)), inc="OD", inck=16, dma=True)
            continue
        sA16q = sA[:].bitcast(F16)
        for sl in range(DC // 2):
            g0 = 2 * sl
            for j in range(2):
                em.op("vector", lambda e, g=g0 + j, j=j, sA16q=sA16q: e.scalar_tensor_tensor(
                    sy0[:, j * C:(j + 1) * C], sA16q[:, g * C:(g + 1) * C], snw[:, g:g + 1],
                    sbc[:, C:2 * C], op0=ALU.mult, op1=ALU.mult), inc="V" if j == 1 else None)
            em.wait("scalar", "V")
            em.op("scalar", lambda e: e.activation(
                sz[:, 0:2 * C], sy0[:, 0:2 * C], AF.Identity, bias=sck[:, 0:1]), inc="S")
            em.wait("vector", "S")
            em.op("vector", lambda e: e.scalar_tensor_tensor(
                sy0[:, 0:2 * C], sz[:, 0:2 * C], -1.0, sy0[:, 0:2 * C], op0=ALU.mult, op1=ALU.add))
            em.op("vector", lambda e, g0=g0: e.scalar_tensor_tensor(
                sA[:, g0 * C:(g0 + 2) * C], sy0[:, 0:2 * C], -127.5, sz[:, 0:2 * C],
                op0=ALU.is_ge, op1=ALU.add), inc="V")
        barrier()

        if dbg == 23:
            em.wait("sync", "V")
            em.op("sync", lambda e, s=seq, c=chs: e.dma_start(
                out=x_out[s, c].rearrange("a p t -> p a t"),
                in_=sx[:].rearrange("p (a t) -> p a t", a=DC)), inc="OD", inck=16, dma=True)
            continue
        if first:
            mask0 = sz[:, 0:512]
            mask1 = sz[:, 512:1024]
            for (wf, wsc, npiece, thp, thn, nslab, width, slot_w) in (
                    (wgu_f, wgu_s, 4, sthr[:, 0:1], sthr[:, 1:2], GUT, 2048, 2048),
                    (wdn_f, wdn_s, 11, sthr[:, 2:3], sthr[:, 3:4], DNT, FFC * 128, FFC * 128)):
                for sl in range(nslab):
                    tslot = sl % 2
                    tsem = "WF%d" % tslot
                    tern = sWF[:, tslot * slot_w:(tslot + 1) * slot_w]
                    em.wait("sync", "WF0")
                    em.wait("sync", "WF1")
                    em.wait("sync", "V")
                    em.wait("vector", "WF0")
                    em.wait("vector", "WF1")
                    src3 = wf[sl].rearrange("a p o -> p a o")
                    for k in range(npiece):
                        ps = k % 2
                        sem = "MW%d" % ps
                        stg = sy0[:, ps * 512:(ps + 1) * 512]
                        msk = mask0 if ps == 0 else mask1
                        if k >= 2:
                            em.wait("sync", "V", mw_state["v"][ps])
                        em.op("sync", lambda e, d=stg, s=src3, k=k: e.dma_start(
                            out=d.rearrange("p (a o) -> p a o", o=128),
                            in_=s[:, 4 * k:4 * (k + 1), :]), inc=sem, inck=16, dma=True)
                        em.wait("vector", sem)
                        em.op("vector", lambda e, stg=stg, msk=msk, thn=thn: e.tensor_scalar(
                            msk, stg, thn, None, op0=ALU.is_lt))
                        em.op("vector", lambda e, t=tern, k=k, stg=stg, msk=msk, thp=thp:
                              e.scalar_tensor_tensor(
                                  t[:, k * 512:(k + 1) * 512], stg, thp, msk,
                                  op0=ALU.is_gt, op1=ALU.subtract), inc="V")
                        mw_state["v"][ps] = em.cnt["V"]
                    em.wait("sync", "V")
                    em.op("sync", lambda e, d=wsc[sl], s=tern: e.dma_start(
                        out=d.rearrange("p a o -> p (a o)"), in_=s), inc=tsem, inck=16, dma=True)

        em.op("vector", lambda e: e.memset(srg[:, 2 * C:3 * C], 0.0), inc="V")
        for j in range(FFC):
            slabs = []
            for which, oti in ((0, j), (1, j + FFC)):
                buf = 2 * ((j % 2)) + which
                sem = "WF%d" % buf
                raw = sWF[:, buf * 2048:(buf + 1) * 2048]
                em.wait("sync", "T", wf_lastT[buf])
                em.wait("sync", sem)
                em.op("sync", lambda e, d=raw, s=wgu_s[oti]: e.dma_start(
                    out=d, in_=s.rearrange("p a o -> p (a o)")), inc=sem, inck=16)
                slabs.append((buf, raw, oti))
            em.wait("tensor", "WF%d" % slabs[0][0])
            em.wait("tensor", "WF%d" % slabs[1][0])
            if first:
                for bi, (buf, raw, oti) in enumerate(slabs):
                    pcs = pS1 if bi == 0 else pS2
                    for dc in range(DC):
                        mm(pcs[:, 0:1], raw[:, dc * 128:(dc + 1) * 128], soc[:],
                           dc == 0, dc == DC - 1, inc="T" if dc == DC - 1 else None)
                em.wait("vector", "T")
                for bi, (buf, raw, oti) in enumerate(slabs):
                    pcs = pS1 if bi == 0 else pS2
                    em.op("vector", lambda e, oti=oti, pcs=pcs: e.tensor_scalar(
                        scol[:, oti:oti + 1], pcs[:, 0:1], -128.0, None, op0=ALU.mult),
                        inc="V" if bi == 1 else None)
                em.wait("tensor", "V")
            gslab, uslab = slabs[0][1], slabs[1][1]
            em.wait("tensor", "S")
            em.wait("tensor", "V")
            for dc in range(DC):
                mm(pA[:, 0:C], gslab[:, dc * 128:(dc + 1) * 128], sA[:, dc * C:(dc + 1) * C],
                   dc == 0, dc == DC - 1)
            for dc in range(DC):
                mm(pB[:, 0:C], uslab[:, dc * 128:(dc + 1) * 128], sA[:, dc * C:(dc + 1) * C],
                   dc == 0, dc == DC - 1, inc="T" if dc == DC - 1 else None)
            wf_lastT[slabs[0][0]] = em.cnt["T"]
            wf_lastT[slabs[1][0]] = em.cnt["T"]
            em.wait("scalar", "T")
            em.op("scalar", lambda e, j=j: e.activation(
                srg[:, 0:C], pA[:, 0:C], AF.Relu, bias=scol[:, j:j + 1]))
            em.op("scalar", lambda e: e.activation(srg[:, C:2 * C], srg[:, 0:C], AF.Square), inc="S")
            em.wait("vector", "S")
            sB16 = sB[:].bitcast(F16)
            em.op("vector", lambda e, j=j: e.scalar_tensor_tensor(
                sy0[:, 0:C], pB[:, 0:C], scol[:, FFC + j:FFC + j + 1],
                srg[:, C:2 * C], op0=ALU.add, op1=ALU.mult))
            em.op("vector", lambda e, j=j, sB16=sB16: e.tensor_tensor(
                sB16[:, j * C:(j + 1) * C], sy0[:, 0:C], sbc[:, 0:C], ALU.mult))
            em.op("vector", lambda e, j=j, sB16=sB16: e.tensor_tensor(
                srg[:, 0:C], sB16[:, j * C:(j + 1) * C], sB16[:, j * C:(j + 1) * C], ALU.mult))
            em.op("vector", lambda e: e.tensor_tensor(
                srg[:, 2 * C:3 * C], srg[:, 2 * C:3 * C], srg[:, 0:C], ALU.max), inc="V")
        barrier()

        if dbg == 24:
            em.wait("sync", "V")
            em.op("sync", lambda e, s=seq, c=chs: e.dma_start(
                out=x_out[s, c].rearrange("a p t -> p a t"),
                in_=sx[:].rearrange("p (a t) -> p a t", a=DC)), inc="OD", inck=16, dma=True)
            continue
        em.op("vector", lambda e: e.tensor_copy(srg[:, 0:C], srg[:, 2 * C:3 * C]), inc="V")
        token_rows(srg[:, 0:C], row(R_MX))
        em.wait("scalar", "V")
        em.op("scalar", lambda e: e.activation(row(R_MX), row(R_MX), AF.Sqrt), inc="S")
        em.wait("vector", "S")
        em.op("vector", lambda e: e.tensor_scalar(row(R_T0), row(R_MX), 1e-5, None, op0=ALU.max))
        em.op("vector", lambda e: e.reciprocal(row(R_S2), row(R_T0)))
        em.op("vector", lambda e: e.tensor_scalar(row(R_S2), row(R_S2), 127.0, None, op0=ALU.mult))
        em.op("vector", lambda e: e.tensor_scalar(
            row(R_OS2), row(R_T0), srw2[:, 1:2], 1.0 / 127.0, op0=ALU.mult, op1=ALU.mult), inc="V")
        em.wait("tensor", "V")
        mm(pBC[:, 0:C], sof[:], row(R_S2), True, True, f32=True)
        mm(pS1[:, 0:C], sof[:], row(R_OS2), True, True, inc="T", f32=True)
        em.wait("vector", "T")
        em.op("vector", lambda e: e.tensor_copy(sbc[:, 0:C], pBC[:, 0:C]))
        em.op("vector", lambda e: e.tensor_copy(sbc[:, C:2 * C], pS1[:, 0:C]), inc="V")
        barrier()

        sB16q = sB[:].bitcast(F16)
        for sl in range(FFC // 2):
            f0 = 2 * sl
            for j in range(2):
                em.op("vector", lambda e, f=f0 + j, j=j, sB16q=sB16q: e.tensor_tensor(
                    sy0[:, j * C:(j + 1) * C], sB16q[:, f * C:(f + 1) * C], sbc[:, 0:C], ALU.mult),
                    inc="V" if j == 1 else None)
            em.wait("scalar", "V")
            em.op("scalar", lambda e: e.activation(
                sz[:, 0:2 * C], sy0[:, 0:2 * C], AF.Identity, bias=sck[:, 0:1]), inc="S")
            em.wait("vector", "S")
            em.op("vector", lambda e: e.scalar_tensor_tensor(
                sy0[:, 0:2 * C], sz[:, 0:2 * C], -1.0, sy0[:, 0:2 * C], op0=ALU.mult, op1=ALU.add))
            em.op("vector", lambda e, f0=f0: e.scalar_tensor_tensor(
                sB[:, f0 * C:(f0 + 2) * C], sy0[:, 0:2 * C], -127.5, sz[:, 0:2 * C],
                op0=ALU.is_ge, op1=ALU.add), inc="V")
        barrier()

        if dbg == 25:
            em.wait("sync", "V")
            em.op("sync", lambda e, s=seq, c=chs: e.dma_start(
                out=x_out[s, c].rearrange("a p t -> p a t"),
                in_=sx[:].rearrange("p (a t) -> p a t", a=DC)), inc="OD", inck=16, dma=True)
            continue
        for ot in range(DNT):
            buf = ot % 2
            raw = sWF[:, buf * FFC * 128:(buf + 1) * FFC * 128]
            sems_cover = ["WF%d" % k for k in range(4)] if ot < 2 else ["WF%d" % buf]
            em.wait("sync", "T", wf_lastT[buf])
            for sm in sems_cover:
                em.wait("sync", sm)
            em.op("sync", lambda e, d=raw, s=wdn_s[ot]: e.dma_start(
                out=d, in_=s.rearrange("p a o -> p (a o)")), inc="WF%d" % buf, inck=16)
            em.wait("tensor", "WF%d" % buf)
            if first:
                for fc in range(FFC):
                    mm(pS1[:, 0:1], raw[:, fc * 128:(fc + 1) * 128], soc[:],
                       fc == 0, fc == FFC - 1, inc="T" if fc == FFC - 1 else None)
                em.wait("vector", "T")
                em.op("vector", lambda e, ot=ot: e.tensor_scalar(
                    scol[:, GUT + ot:GUT + ot + 1], pS1[:, 0:1], -128.0, None, op0=ALU.mult), inc="V")
                em.wait("tensor", "V")
            pb = pA if ot % 2 == 0 else pB
            em.wait("tensor", "V")
            for fc in range(FFC):
                mm(pb[:, 0:C], raw[:, fc * 128:(fc + 1) * 128], sB[:, fc * C:(fc + 1) * C],
                   fc == 0, fc == FFC - 1, inc="T" if fc == FFC - 1 else None)
            wf_lastT[buf] = em.cnt["T"]
            em.wait("vector", "T")
            em.op("vector", lambda e, ot=ot, pb=pb: e.scalar_tensor_tensor(
                srg[:, 0:C], pb[:, 0:C], scol[:, GUT + ot:GUT + ot + 1],
                sbc[:, C:2 * C], op0=ALU.add, op1=ALU.mult))
            em.op("vector", lambda e, ot=ot: e.tensor_tensor(
                sx[:, ot * C:(ot + 1) * C], sx[:, ot * C:(ot + 1) * C], srg[:, 0:C], ALU.add), inc="V")
        barrier()

        em.op("sync", lambda e, s=seq, c=chs: e.dma_start(
            out=x_out[s, c].rearrange("a p t -> p a t"),
            in_=sx[:].rearrange("p (a t) -> p a t", a=DC)), inc="OD", inck=16)

    for nm in ("OD", "XL", "CO", "W1_0", "W1_1", "WF0", "WF1", "WF2", "WF3"):
        em.wait("sync", nm)
    for nm in ("CR", "MW0", "MW1", "W1_0G", "W1_1G", "WF0G", "WF1G", "WF2G", "WF3G"):
        em.wait("gpsimd", nm)

    with ctx:
        with nc.Block() as block:
            @block.sync
            def _(e):
                for fn in em.q["sync"]:
                    fn(e)

            @block.tensor
            def _(e):
                for fn in em.q["tensor"]:
                    fn(e)

            @block.vector
            def _(e):
                for fn in em.q["vector"]:
                    fn(e)

            @block.scalar
            def _(e):
                for fn in em.q["scalar"]:
                    fn(e)

            @block.gpsimd
            def _(e):
                for fn in em.q["gpsimd"]:
                    fn(e)

    return nc


def _prep(inputs):
    x = np.asarray(inputs["L0_attn_x"], np.float32)
    Wq = np.asarray(inputs["L0_attn_Wq"], np.float32)
    Wk = np.asarray(inputs["L0_attn_Wk"], np.float32)
    Wv = np.asarray(inputs["L0_attn_Wv"], np.float32)
    Wo = np.asarray(inputs["L0_attn_Wo"], np.float32)
    Wbeta = np.asarray(inputs["L0_attn_Wbeta"], np.float32)
    Wg1 = np.asarray(inputs["L0_attn_Wg1"], np.float32)
    Wg2 = np.asarray(inputs["L0_attn_Wg2"], np.float32)
    gnorm = np.asarray(inputs["L0_attn_g_norm"], np.float32)
    wgu = np.asarray(inputs["L0_ffn_w_gate_up"], np.float32)
    wdn = np.asarray(inputs["L0_ffn_w_down"], np.float32)
    nw = np.asarray(inputs["L0_ffn_norm_weight"], np.float32)

    def wT(W):
        return np.ascontiguousarray(
            W.T.reshape(DC, 128, W.shape[0] // 128, 128).transpose(2, 0, 1, 3))

    shared = dict(
        wq_f=wT(Wq), wk_f=wT(Wk), wv_f=wT(Wv), wo_f=wT(Wo), wgu_f=wT(wgu),
        wdn_f=np.ascontiguousarray(wdn.T.reshape(FFC, 128, DNT, 128).transpose(2, 0, 1, 3)),
        wbeta_f=np.ascontiguousarray(Wbeta.T.reshape(DC, 128, H).transpose(1, 0, 2)),
        wg1_f=np.ascontiguousarray(Wg1.T.reshape(DC, 128, 128)),
        wg2_f=np.ascontiguousarray(Wg2.T),
        gnorm_f=np.ascontiguousarray(gnorm.reshape(DC, 128).T),
        normw_f=np.ascontiguousarray(nw.reshape(DC, 128).T),
        c_id=np.eye(128, dtype=np.float32).astype(BF),
        c_nid=(-np.eye(128, dtype=np.float32)).astype(BF),
        c_mu=np.triu(np.ones((128, 128), np.float32), 1).astype(BF),
        c_mud=np.triu(np.ones((128, 128), np.float32), 0).astype(BF),
        c_ones1=np.ones((1, 128), np.float32).astype(BF),
        c_onescol=np.ones((128, 1), np.float32).astype(BF),
        c_idf=np.eye(128, dtype=np.float32),
    )
    in_maps = []
    for c in range(NCORES):
        xc = x[SEQS * c:SEQS * (c + 1)]
        x_fm = np.ascontiguousarray(
            xc.transpose(0, 2, 1).reshape(SEQS, DC, 128, N // C, C).transpose(0, 3, 1, 2, 4))
        m = dict(shared)
        m["x_in"] = x_fm
        in_maps.append(m)
    return in_maps


def kernel(**inputs):
    in_maps = _prep(inputs)
    nc = build_nc()
    res = run_bass_kernel_spmd(nc, in_maps, core_ids=list(range(NCORES)))
    out = np.empty((NCORES * SEQS, N, D), np.float32)
    for c in range(NCORES):
        o = np.asarray(res.results[c]["x_out"])
        o = o.transpose(0, 2, 3, 1, 4).reshape(SEQS, D, N).transpose(0, 2, 1)
        out[SEQS * c:SEQS * (c + 1)] = o
    return out


# version 07 (final)
# speedup vs baseline: 1.0003x; 1.0003x over previous
import contextlib

import numpy as np
import ml_dtypes

import concourse.bass as bass
import concourse.mybir as mybir
from concourse.bass_utils import run_bass_kernel_spmd

F32 = mybir.dt.float32
BF16 = mybir.dt.bfloat16
I32 = mybir.dt.int32
I16 = mybir.dt.int16
F16 = mybir.dt.float16
AF = mybir.ActivationFunctionType
ALU = mybir.AluOpType
AX = mybir.AxisListType
BF = ml_dtypes.bfloat16

NCORES = 8
SEQS = 2
N = 2048
D = 2048
H = 16
HD = 128
C = 512
NCH = SEQS * N // C
TAU = C // 128
DC = D // 128
FF = 5632
FO = 2 * FF
GUT = FO // 128
FFC = FF // 128
DNT = D // 128
NSLAB = GUT + DNT


class Em:
    ENGS = ("sync", "tensor", "vector", "scalar", "gpsimd")

    def __init__(self):
        self.q = {e: [] for e in self.ENGS}
        self.sem = {}
        self.cnt = {}

    def add_sem(self, name, handle):
        self.sem[name] = handle
        self.cnt[name] = 0

    def wait(self, eng, name, n=None):
        if n is None:
            n = self.cnt[name]
        if n <= 0:
            return
        h = self.sem[name]
        self.q[eng].append(lambda e, h=h, n=n: e.wait_ge(h, n))

    OWN = {"vector": "V", "scalar": "S"}

    def op(self, eng, fn, inc=None, inck=1, dma=False):
        own = None if dma else self.OWN.get(eng)
        if own is not None:
            prev = self.cnt[own]
            if prev > 0:
                h0 = self.sem[own]
                self.q[eng].append(lambda e, h=h0, n=prev: e.wait_ge(h, n))
            inc, inck = own, 1
        if inc is not None:
            h = self.sem[inc]
            self.cnt[inc] += inck

            def run(e, fn=fn, h=h, k=inck):
                fn(e).then_inc(h, k)
        else:
            def run(e, fn=fn):
                fn(e)
        self.q[eng].append(run)


def build_nc(n_chunks=NCH, dbg=0):
    nc = bass.Bass("TRN2", target_bir_lowering=False)
    dt = nc.dram_tensor
    x_in = dt("x_in", [SEQS, N // C, DC, 128, C], F32, kind="ExternalInput").ap()
    x_out = dt("x_out", [SEQS, N // C, DC, 128, C], F32, kind="ExternalOutput").ap()
    wq_f = dt("wq_f", [DC, DC, 128, 128], F32, kind="ExternalInput").ap()
    wk_f = dt("wk_f", [DC, DC, 128, 128], F32, kind="ExternalInput").ap()
    wv_f = dt("wv_f", [DC, DC, 128, 128], F32, kind="ExternalInput").ap()
    wo_f = dt("wo_f", [DC, DC, 128, 128], F32, kind="ExternalInput").ap()
    wbeta_f = dt("wbeta_f", [128, DC, H], F32, kind="ExternalInput").ap()
    wg1_f = dt("wg1_f", [DC, 128, 128], F32, kind="ExternalInput").ap()
    wg2_f = dt("wg2_f", [128, D], F32, kind="ExternalInput").ap()
    gnorm_f = dt("gnorm_f", [128, DC], F32, kind="ExternalInput").ap()
    normw_f = dt("normw_f", [128, DC], F32, kind="ExternalInput").ap()
    wgu_f = dt("wgu_f", [GUT, DC, 128, 128], F32, kind="ExternalInput").ap()
    wdn_f = dt("wdn_f", [DNT, FFC, 128, 128], F32, kind="ExternalInput").ap()
    c_id = dt("c_id", [128, 128], BF16, kind="ExternalInput").ap()
    c_nid = dt("c_nid", [128, 128], BF16, kind="ExternalInput").ap()
    c_mu = dt("c_mu", [128, 128], BF16, kind="ExternalInput").ap()
    c_mud = dt("c_mud", [128, 128], BF16, kind="ExternalInput").ap()
    c_ones1 = dt("c_ones1", [1, 128], BF16, kind="ExternalInput").ap()
    c_onescol = dt("c_onescol", [128, 1], BF16, kind="ExternalInput").ap()
    c_idf = dt("c_idf", [128, 128], F32, kind="ExternalInput").ap()
    wq_s = dt("wq_s", [DC, 128, DC, 128], BF16, kind="Internal").ap()
    wk_s = dt("wk_s", [DC, 128, DC, 128], BF16, kind="Internal").ap()
    wv_s = dt("wv_s", [DC, 128, DC, 128], BF16, kind="Internal").ap()
    wo_s = dt("wo_s", [DC, 128, DC, 128], BF16, kind="Internal").ap()
    wgu_s = dt("wgu_s", [GUT, 128, DC, 128], BF16, kind="Internal").ap()
    wdn_s = dt("wdn_s", [DNT, 128, FFC, 128], BF16, kind="Internal").ap()

    ctx = contextlib.ExitStack()
    sb = lambda name, shape, dtp: ctx.enter_context(nc.sbuf_tensor(name, shape, dtp))
    ps = lambda name, shape, dtp: ctx.enter_context(nc.psum_tensor(name, shape, dtp))

    sx = sb("sx", [128, DC * C], F32)
    sA = sb("sA", [128, DC * C], BF16)
    sB = sb("sB", [128, FFC * C], BF16)
    sQKV = sb("sQKV", [128, 3 * C], BF16)
    sO = sb("sO", [128, DC * C], BF16)
    sS = sb("sS", [128, H * HD], F32)
    sSb = sb("sSb", [128, H * HD], BF16)
    sW1 = sb("sW1", [128, 2 * D], BF16)
    sWF = sb("sWF", [128, 12288], BF16)
    sWg1 = sb("sWg1", [128, D], BF16)
    sWg2 = sb("sWg2", [128, D], BF16)
    sWbeta = sb("sWbeta", [128, DC * H], BF16)
    sgnn = sb("sgnn", [128, DC], F32)
    snw = sb("snw", [128, DC], F32)
    scol = sb("scol", [128, GUT + DNT], F32)
    sbc = sb("sbc", [128, 2 * C], F32)
    sbb = sb("sbb", [16, C], BF16)
    sbT = sb("sbT", [128, (C // 64) * H], F32)
    sTN = sb("sTN", [128, 2 * 64], F32)
    sTP = sb("sTP", [128, 2 * 64], F32)
    sEf = sb("sEf", [128, 128], F32)
    sD = sb("sD", [128, 128], BF16)
    sHm = sb("sHm", [128, 64], BF16)
    sKt = sb("sKt", [128, 128], BF16)
    sy0 = sb("sy0", [128, 2 * C], F32)
    sz = sb("sz", [128, 2 * C], I16)
    ssm = sb("ssm", [128, 2 * C], F32)
    smxb = sb("smxb", [128, TAU], BF16)
    srg = sb("srg", [128, 3 * C], BF16)
    sg1 = sb("sg1", [128, C], BF16)
    ssg = sb("ssg", [128, C], BF16)
    srow = sb("srow", [1, 9 * C], F32)
    srw2 = sb("srw2", [1, 4], F32)
    sacc = sb("sacc", [128, 12], F32)
    saccb = sb("saccb", [128, 6], BF16)
    sck = sb("sck", [128, 8], F32)
    sthr = sb("sthr", [128, 4], F32)
    sid = sb("sid", [128, 128], BF16)
    sidf = sb("sidf", [128, 128], F32)
    snid = sb("snid", [128, 128], BF16)
    smu = sb("smu", [128, 128], BF16)
    smud = sb("smud", [128, 128], BF16)
    sof = sb("sof", [1, 128], F32)
    soc = sb("soc", [128, 1], BF16)

    pA = ps("pA", [128, 512], F32)
    pB = ps("pB", [128, 512], F32)
    pS1 = ps("pS1", [128, 512], F32)
    pS2 = ps("pS2", [128, 512], F32)
    pBC = ps("pBC", [128, 512], F32)
    pE = ps("pE", [128, 512], F32)
    pGH = ps("pGH", [128, 1024], BF16)
    pX = ps("pX", [128, 512], F32)

    em = Em()
    for s in ("T", "V", "S", "OD", "XL", "CO", "CR", "MW0", "MW1",
              "W1_0", "W1_1", "WF0", "WF1", "WF2", "WF3",
              "W1_0G", "W1_1G", "WF0G", "WF1G", "WF2G", "WF3G"):
        em.add_sem(s, ctx.enter_context(nc.semaphore("sem" + s)))

    (R_MEAN, R_VAR, R_T0, R_T1, R_RHO, R_MX, R_C, R_C3, R_S2) = range(9)
    R_ALPHA = R_MX
    R_EX2 = R_T1
    R_RSTD = R_S2
    R_OS2 = R_T1

    def row(i):
        return srow[:, i * C:(i + 1) * C]

    def mm(out, lhsT, rhs, start, stop, inc=None, f32=False):
        if f32 and inc:
            em.op("tensor", lambda e, o=out, l=lhsT, r=rhs, st=start, sp=stop:
                  e.matmul(o, l, r, start=st, stop=sp))
            em.op("tensor", lambda e: e.matmul(pX[0:1, 0:1], soc[:], soc[:],
                                               start=True, stop=True), inc=inc)
            return
        em.op("tensor", lambda e, o=out, l=lhsT, r=rhs, st=start, sp=stop:
              e.matmul(o, l, r, start=st, stop=sp), inc=inc)

    def barrier():
        mm(pX[0:1, 0:1], soc[:], soc[:], True, True, inc="T")
        for e, s in (("vector", "V"), ("scalar", "S")):
            em.op(e, lambda x: x.nop(), inc=s)
        snap = {s: em.cnt[s] for s in ("T", "V", "S")}
        for e in ("tensor", "vector", "scalar", "sync", "gpsimd"):
            for s, n in snap.items():
                em.wait(e, s, n)

    for i, val in [(0, 128.0), (1, 1e-6), (2, 1e-5)]:
        em.op("vector", lambda e, i=i, v=val: e.memset(sck[:, i:i + 1], v))
    em.op("vector", lambda e: e.memset(sS[:], 0.0))
    em.op("vector", lambda e: e.memset(sSb[:], 0.0))
    em.op("vector", lambda e: e.memset(sacc[:], 0.0))
    em.op("vector", lambda e: e.memset(sof[:], 1.0), inc="V")
    em.op("sync", lambda e: e.dma_start(out=sid[:], in_=c_id), inc="CO", inck=16)
    em.op("sync", lambda e: e.dma_start(out=sidf[:], in_=c_idf), inc="CO", inck=16)
    em.op("sync", lambda e: e.dma_start(out=snid[:], in_=c_nid), inc="CO", inck=16)
    em.op("sync", lambda e: e.dma_start(out=smu[:], in_=c_mu), inc="CO", inck=16)
    em.op("sync", lambda e: e.dma_start(out=smud[:], in_=c_mud), inc="CO", inck=16)
    em.op("sync", lambda e: e.dma_start(out=soc[:], in_=c_onescol), inc="CO", inck=16)
    em.op("sync", lambda e: e.dma_start(out=sgnn[:], in_=gnorm_f), inc="CO", inck=16)
    em.op("sync", lambda e: e.dma_start(out=snw[:], in_=normw_f), inc="CO", inck=16)
    em.op("gpsimd", lambda e: e.dma_start(
        out=sWg1[:].rearrange("p (a o) -> p a o", a=DC),
        in_=wg1_f.rearrange("a p o -> p a o")), inc="CR", inck=16)
    em.op("gpsimd", lambda e: e.dma_start(out=sWg2[:], in_=wg2_f), inc="CR", inck=16)
    em.op("gpsimd", lambda e: e.dma_start(
        out=sWbeta[:], in_=wbeta_f.rearrange("p a h -> p (a h)")), inc="CR", inck=16)
    em.wait("vector", "CO")
    em.op("vector", lambda e: e.tensor_scalar(sgnn[:], sgnn[:], -1.0, None, op0=ALU.mult), inc="V")
    v_static = em.cnt["V"]
    em.wait("tensor", "V", v_static)
    em.wait("scalar", "V", v_static)
    em.wait("tensor", "CO")
    em.wait("scalar", "CO")
    em.wait("tensor", "CR")

    mw_pieces = []
    for i in range(GUT):
        g3 = wgu_f[i].rearrange("a p o -> p a o")
        for k in range(4):
            mw_pieces.append((g3[:, 4 * k:4 * (k + 1), :], 512, 0))
    for ot in range(DNT):
        dn3 = wdn_f[ot].rearrange("a p o -> p a o")
        for k in range(11):
            mw_pieces.append((dn3[:, 4 * k:4 * (k + 1), :], 512, 1))
    NQ = len(mw_pieces)
    mw_state = {"i": 0, "s": [0] * NQ, "v": [0, 0]}

    def meanw_quanta(k):
        for _ in range(k):
            i = mw_state["i"]
            if i >= NQ:
                return
            slot = i % 2
            sem = "MW%d" % slot
            src, width, a = mw_pieces[i]
            dstp = sy0[:, slot * 512:slot * 512 + width]
            if i >= 2:
                em.wait("sync", "S", mw_state["s"][i - 2])
            em.op("sync", lambda e, d=dstp, s=src, w=width: e.dma_start(
                out=d.rearrange("p (a o) -> p a o", o=128), in_=s), inc=sem, inck=16, dma=True)
            em.wait("scalar", sem)
            if i > 0:
                em.wait("scalar", "V")
            em.op("scalar", lambda e, s=dstp: e.activation(
                s, s, AF.Abs, accum_out=sacc[:, 4:5]), inc="S")
            mw_state["s"][i] = em.cnt["S"]
            em.wait("vector", "S", mw_state["s"][i])
            em.op("vector", lambda e, a=a: e.tensor_tensor(
                sacc[:, 5:6], sacc[:, 4:5], sacc[:, 2 + a:3 + a], ALU.subtract))
            em.op("vector", lambda e, a=a: e.tensor_tensor(
                sacc[:, 6:7], sacc[:, a:a + 1], sacc[:, 5:6], ALU.add))
            em.op("vector", lambda e, a=a: e.tensor_tensor(
                sacc[:, 7:8], sacc[:, 6:7], sacc[:, a:a + 1], ALU.subtract))
            em.op("vector", lambda e, a=a: e.tensor_tensor(
                sacc[:, 2 + a:3 + a], sacc[:, 7:8], sacc[:, 5:6], ALU.subtract))
            em.op("vector", lambda e, a=a: e.tensor_copy(
                sacc[:, a:a + 1], sacc[:, 6:7]), inc="V")
            mw_state["i"] = i + 1

    w1_lastT = [0, 0]
    wf_lastT = [0, 0, 0, 0]

    def ternarize_region(src, dst, width, thp, thn):
        lo = 0
        while lo < width:
            hi = min(lo + 1024, width)
            em.op("vector", lambda e, lo=lo, hi=hi, thn=thn, src=src: e.tensor_scalar(
                sy0[:, 0:hi - lo], src[:, lo:hi], thn, None, op0=ALU.is_lt))
            em.op("vector", lambda e, lo=lo, hi=hi, thp=thp, src=src, dst=dst: e.scalar_tensor_tensor(
                dst[:, lo:hi], src[:, lo:hi], thp, sy0[:, 0:hi - lo],
                op0=ALU.is_gt, op1=ALU.subtract), inc="V" if hi == width else None)
            lo = hi

    for ch in range(n_chunks):
        first = ch == 0
        seq, chs = divmod(ch, N // C)
        if ch % (N // C) == 0 and ch > 0:
            em.op("vector", lambda e: e.memset(sS[:], 0.0))
            em.op("vector", lambda e: e.memset(sSb[:], 0.0), inc="V")
            em.wait("tensor", "V")

        if ch > 0:
            em.wait("sync", "OD")
        em.op("sync", lambda e, s=seq, c=chs: e.dma_start(
            out=sx[:].rearrange("p (a t) -> p a t", a=DC),
            in_=x_in[s, c].rearrange("a p t -> p a t")), inc="XL", inck=16)
        xl_x = em.cnt["XL"]

        if dbg == 3:
            barrier()
            em.wait("sync", "V")
            em.op("sync", lambda e, s=seq, c=chs: e.dma_start(
                out=x_out[s, c].rearrange("a p t -> p a t"),
                in_=sx[:].rearrange("p (a t) -> p a t", a=DC)), inc="OD", inck=16, dma=True)
            continue
        def ln_stats():
            em.wait("scalar", "XL", xl_x)
            tprev = [0, 0]
            bufs = [(srg[:, 0:C], srg[:, C:2 * C]), (srg[:, 2 * C:3 * C], ssg[:, 0:C])]
            for g in range(DC):
                b = g % 2
                xb, xs = bufs[b]
                if g >= 2:
                    em.wait("scalar", "T", tprev[b])
                em.op("scalar", lambda e, o=xb, g=g: e.activation(
                    o, sx[:, g * C:(g + 1) * C], AF.Identity), inc="S")
                em.wait("tensor", "S")
                mm(pS1[0:1, 0:C], soc[:], xb, g == 0, g == DC - 1)
                em.op("scalar", lambda e, o=xs, i=xb: e.activation(o, i, AF.Square), inc="S")
                em.wait("tensor", "S")
                mm(pS2[0:1, 0:C], soc[:], xs, g == 0, g == DC - 1, inc="T")
                tprev[b] = em.cnt["T"]
            em.wait("vector", "T")
            em.op("vector", lambda e: e.tensor_scalar(row(R_MEAN), pS1[0:1, 0:C], 1.0 / D, None, op0=ALU.mult))
            em.op("vector", lambda e: e.tensor_scalar(row(R_EX2), pS2[0:1, 0:C], 1.0 / D, None, op0=ALU.mult))
            em.op("vector", lambda e: e.tensor_tensor(row(R_VAR), row(R_MEAN), row(R_MEAN), ALU.mult))
            em.op("vector", lambda e: e.tensor_tensor(row(R_VAR), row(R_EX2), row(R_VAR), ALU.subtract), inc="V")

        ln_stats()
        em.wait("scalar", "V")
        em.op("scalar", lambda e: e.activation(row(R_RSTD), row(R_VAR), AF.Sqrt, bias=sck[0:1, 2:3]), inc="S")
        em.wait("vector", "S")
        em.op("vector", lambda e: e.reciprocal(row(R_RSTD), row(R_RSTD)))
        em.op("vector", lambda e: e.tensor_tensor(row(R_T0), row(R_MEAN), row(R_RSTD), ALU.mult), inc="V")
        em.wait("tensor", "V")
        mm(pBC[:, 0:C], sof[:], row(R_RSTD), True, True, f32=True)
        mm(pS1[:, 0:C], sof[:], row(R_T0), True, True, inc="T", f32=True)
        em.wait("vector", "T")
        em.op("vector", lambda e: e.tensor_copy(sbc[:, 0:C], pBC[:, 0:C]))
        em.op("vector", lambda e: e.tensor_copy(sbc[:, C:2 * C], pS1[:, 0:C]), inc="V")
        for g in range(DC):
            em.op("vector", lambda e, g=g: e.tensor_tensor(
                sA[:, g * C:(g + 1) * C], sx[:, g * C:(g + 1) * C], sbc[:, 0:C], ALU.mult))
            em.op("vector", lambda e, g=g: e.tensor_tensor(
                sA[:, g * C:(g + 1) * C], sA[:, g * C:(g + 1) * C], sbc[:, C:2 * C], ALU.subtract),
                inc="V" if g == DC - 1 else None)
        barrier()
        if dbg == 2:
            em.wait("sync", "V")
            em.op("sync", lambda e, s=seq, c=chs: e.dma_start(
                out=x_out[s, c].rearrange("a p t -> p a t"),
                in_=sx[:].rearrange("p (a t) -> p a t", a=DC)), inc="OD", inck=16, dma=True)
            continue

        for dc in range(DC):
            mm(pS2[0:16, 0:C], sWbeta[:, dc * H:(dc + 1) * H], sA[:, dc * C:(dc + 1) * C],
               dc == 0, dc == DC - 1, inc="T" if dc == DC - 1 else None)
        em.wait("scalar", "T")
        em.op("scalar", lambda e: e.activation(sbb[:], pS2[0:16, 0:C], AF.Sigmoid), inc="S")
        em.wait("vector", "S")
        em.op("vector", lambda e: e.tensor_scalar(sbb[:], sbb[:], 2.0, None, op0=ALU.mult), inc="V")
        em.wait("tensor", "V")
        NB64 = C // 64
        for t in range(NB64):
            em.op("tensor", lambda e, t=t: e.transpose(
                pGH[0:64, t * H:(t + 1) * H], sbb[:, t * 64:(t + 1) * 64], sid[0:16, 0:16]),
                inc="T" if t == NB64 - 1 else None)
        em.wait("vector", "T")
        em.op("vector", lambda e: e.tensor_scalar(
            sbT[0:64, 0:NB64 * H], pGH[0:64, 0:NB64 * H], -1.0, None, op0=ALU.mult), inc="V")
        v_beta = em.cnt["V"]
        for dc in range(DC):
            mm(pA[:, 0:C], sWg1[:, dc * 128:(dc + 1) * 128], sA[:, dc * C:(dc + 1) * C],
               dc == 0, dc == DC - 1, inc="T" if dc == DC - 1 else None)
        em.wait("scalar", "T")
        em.op("scalar", lambda e: e.activation(sg1[:], pA[:, 0:C], AF.Identity), inc="S")
        s_g1 = em.cnt["S"]
        em.wait("tensor", "S", s_g1)
        if dbg == 5:
            barrier()
            em.wait("sync", "V")
            em.op("sync", lambda e, s=seq, c=chs: e.dma_start(
                out=x_out[s, c].rearrange("a p t -> p a t"),
                in_=sx[:].rearrange("p (a t) -> p a t", a=DC)), inc="OD", inck=16, dma=True)
            continue

        for h in range(1 if dbg in (1,2,3,4,5,6,7,8,9,10,11,12,13,14,26,27,28,29,30,31,32,33,35,40,41,42) else H):
            if first and dbg != 4:
                meanw_quanta(36 if h < 15 else 10 ** 6)

            def wslice(idx, wsrc_s, wsrc_f, ot):
                buf = idx % 2
                sem = "W1_%d" % buf
                dst = sW1[:, buf * D:(buf + 1) * D]
                if first:
                    em.wait("gpsimd", "T", w1_lastT[buf])
                    em.wait("gpsimd", sem)
                    em.op("gpsimd", lambda e, d=dst, s=wsrc_f[ot]: e.dma_start(
                        out=d.rearrange("p (a o) -> p a o", a=DC),
                        in_=s.rearrange("a p o -> p a o")), inc=sem + "G", inck=16)
                    ldc = em.cnt[sem + "G"]
                    em.wait("sync", sem + "G", ldc)
                    em.op("sync", lambda e, d=wsrc_s[ot], s=dst: e.dma_start(
                        out=d.rearrange("p a o -> p (a o)"), in_=s), inc=sem, inck=16)
                    em.wait("tensor", sem + "G", ldc)
                else:
                    em.wait("sync", "T", w1_lastT[buf])
                    em.wait("sync", sem)
                    em.op("sync", lambda e, d=dst, s=wsrc_s[ot]: e.dma_start(
                        out=d, in_=s.rearrange("p a o -> p (a o)")), inc=sem, inck=16)
                    em.wait("tensor", sem)
                return dst

            qh, kh, vh = sQKV[:, 0:C], sQKV[:, C:2 * C], sQKV[:, 2 * C:3 * C]
            wqd = wslice(3 * h, wq_s, wq_f, h)
            for dc in range(DC):
                mm(pA[:, 0:C], wqd[:, dc * 128:(dc + 1) * 128], sA[:, dc * C:(dc + 1) * C],
                   dc == 0, dc == DC - 1, inc="T" if dc == DC - 1 else None)
            w1_lastT[3 * h % 2] = em.cnt["T"]
            em.wait("scalar", "T")
            em.op("scalar", lambda e, o=qh: e.activation(o, pA[:, 0:C], AF.Silu), inc="S")
            s_q = em.cnt["S"]
            wkd = wslice(3 * h + 1, wk_s, wk_f, h)
            for dc in range(DC):
                mm(pB[:, 0:C], wkd[:, dc * 128:(dc + 1) * 128], sA[:, dc * C:(dc + 1) * C],
                   dc == 0, dc == DC - 1, inc="T" if dc == DC - 1 else None)
            w1_lastT[(3 * h + 1) % 2] = em.cnt["T"]
            em.wait("scalar", "T")
            em.op("scalar", lambda e, o=kh: e.activation(o, pB[:, 0:C], AF.Silu), inc="S")
            wvd = wslice(3 * h + 2, wv_s, wv_f, h)
            em.wait("tensor", "S", s_q)
            for dc in range(DC):
                mm(pA[:, 0:C], wvd[:, dc * 128:(dc + 1) * 128], sA[:, dc * C:(dc + 1) * C],
                   dc == 0, dc == DC - 1, inc="T" if dc == DC - 1 else None)
            w1_lastT[(3 * h + 2) % 2] = em.cnt["T"]
            em.wait("scalar", "T")
            em.op("scalar", lambda e, o=vh: e.activation(o, pA[:, 0:C], AF.Identity, scale=-1.0), inc="S")
            s_proj = em.cnt["S"]
            if dbg == 7:
                continue
            em.wait("vector", "S", s_proj - 1)
            em.op("vector", lambda e, k=kh: e.tensor_tensor(srg[:, 0:C], k, k, ALU.mult), inc="V")
            if dbg == 8:
                continue
            em.wait("tensor", "V")
            mm(pS1[0:1, 0:C], soc[:], srg[:, 0:C], True, True, inc="T")
            em.wait("scalar", "T")
            em.op("scalar", lambda e: e.activation(row(R_T0), pS1[0:1, 0:C], AF.Sqrt, bias=sck[0:1, 1:2]), inc="S")
            if dbg == 9:
                continue
            em.wait("vector", "S")
            em.op("vector", lambda e: e.reciprocal(row(R_T0), row(R_T0)), inc="V")
            if dbg == 10:
                continue
            em.wait("tensor", "V")
            mm(pBC[:, 0:C], sof[:], row(R_T0), True, True, inc="T", f32=True)
            if dbg == 11:
                continue
            em.wait("vector", "T")
            if dbg == 12:
                em.op("vector", lambda e, k=kh: e.tensor_tensor(srg[:, 0:C], k, pBC[:, 0:C], ALU.mult), inc="V")
                continue
            if dbg == 13:
                em.op("scalar", lambda e: e.activation(srg[:, C:2 * C].bitcast(F32) if False else srg[:, 0:C], pBC[:, 0:C], AF.Identity), inc="S")
                em.wait("vector", "S")
                em.op("vector", lambda e, k=kh: e.tensor_tensor(k, k, srg[:, 0:C], ALU.mult), inc="V")
                continue
            em.op("vector", lambda e, k=kh: e.tensor_tensor(k, k, pBC[:, 0:C], ALU.mult), inc="V")
            em.wait("tensor", "V")
            em.wait("tensor", "S", s_proj)
            em.wait("tensor", "V", v_beta)
            if dbg == 6:
                continue

            NLV = 6
            for blk in range(C // 64):
                bsl = slice(blk * 64, (blk + 1) * 64)
                Sh = sS[:, h * HD:(h + 1) * HD]
                Snb = sSb[:, h * HD:(h + 1) * HD]
                bn = sbT[0:64, blk * H + h:blk * H + h + 1]
                mm(pS2[0:64, 0:64], kh[:, bsl], kh[:, bsl], True, True)
                if dbg == 26:
                    em.op("tensor", lambda e: e.matmul(pX[0:1, 0:1], soc[:], soc[:], start=True, stop=True), inc="T")
                    break
                mm(pS1[0:64, 0:64], kh[:, bsl], qh[:, bsl], True, True)
                mm(pE[0:64, 128:256], vh[:, bsl], snid[:], True, False)
                mm(pE[0:64, 128:256], kh[:, bsl], Snb, False, True, inc="T")
                if dbg == 27:
                    break
                em.wait("scalar", "T")
                em.op("scalar", lambda e: e.activation(sEf[0:64, :], pE[0:64, 128:256], AF.Identity), inc="S")
                if dbg == 28:
                    break
                em.wait("vector", "T")
                if dbg == 29:
                    em.op("vector", lambda e, bn=bn: e.scalar_tensor_tensor(
                        sTN[0:64, 0:64], pS2[0:64, 0:64], bn, smu[0:64, 0:64], op0=ALU.mult, op1=ALU.mult), inc="V")
                    break
                if dbg == 40:
                    em.op("vector", lambda e: e.scalar_tensor_tensor(
                        sTN[0:64, 0:64], pS2[0:64, 0:64], -1.0, smu[0:64, 0:64], op0=ALU.mult, op1=ALU.mult), inc="V")
                    break
                if dbg == 41:
                    em.op("vector", lambda e: e.tensor_tensor(
                        sTN[0:64, 0:64], pS2[0:64, 0:64], smu[0:64, 0:64], ALU.mult), inc="V")
                    break
                if dbg == 42:
                    em.op("vector", lambda e: e.tensor_copy(sTN[0:64, 0:64], pS2[0:64, 0:64]), inc="V")
                    break
                if dbg == 35:
                    em.op("vector", lambda e: e.tensor_tensor(
                        sHm[0:64, 0:64], pS1[0:64, 0:64], smud[0:64, 0:64], ALU.mult), inc="V")
                    break
                em.op("vector", lambda e, bn=bn: e.scalar_tensor_tensor(
                    sTN[0:64, 0:64], pS2[0:64, 0:64], bn, smu[0:64, 0:64], op0=ALU.mult, op1=ALU.mult))
                em.op("vector", lambda e: e.tensor_tensor(
                    sHm[0:64, 0:64], pS1[0:64, 0:64], smud[0:64, 0:64], ALU.mult), inc="V")
                if dbg == 30:
                    break
                em.wait("tensor", "V")
                em.op("tensor", lambda e: e.transpose(pS1[0:64, 0:64], sTN[0:64, 0:64], sidf[0:64, 0:64]))
                em.op("tensor", lambda e: e.matmul(pX[0:1, 0:1], soc[:], soc[:],
                                                   start=True, stop=True), inc="T")
                em.wait("scalar", "T")
                em.op("scalar", lambda e: e.activation(sTP[0:64, 0:64], pS1[0:64, 0:64], AF.Identity), inc="S")
                if dbg == 31:
                    break
                em.wait("tensor", "S")
                for lv in range(NLV):
                    pp = lv % 2
                    npp = 1 - pp
                    tn = sTN[0:64, pp * 64:(pp + 1) * 64]
                    tp = sTP[0:64, pp * 64:(pp + 1) * 64]
                    tnn = sTN[0:64, npp * 64:(npp + 1) * 64]
                    tpn = sTP[0:64, npp * 64:(npp + 1) * 64]
                    mm(pE[0:64, 128:256], tn, sEf[0:64, :], True, True,
                       inc=None if lv < NLV - 1 else "T", f32=True)
                    if lv < NLV - 1:
                        mm(pS1[0:64, 0:64], tn, tp, True, True, f32=True)
                        mm(pS2[0:64, 0:64], tp, tn, True, True, inc="T", f32=True)
                    em.wait("scalar", "T")
                    em.wait("vector", "T")
                    if lv < NLV - 1:
                        em.op("scalar", lambda e, d=tpn: e.activation(
                            d, pS1[0:64, 0:64], AF.Identity))
                        em.op("scalar", lambda e, d=tnn: e.activation(
                            d, pS2[0:64, 0:64], AF.Identity), inc="S")
                        em.op("vector", lambda e: e.tensor_tensor(
                            sEf[0:64, :], sEf[0:64, :], pE[0:64, 128:256], ALU.add), inc="V")
                        em.wait("tensor", "S")
                        em.wait("tensor", "V")
                    else:
                        em.op("vector", lambda e: e.tensor_tensor(
                            sEf[0:64, :], sEf[0:64, :], pE[0:64, 128:256], ALU.add))
                        em.op("vector", lambda e, bn=bn: e.tensor_scalar(
                            sD[0:64, :], sEf[0:64, :], bn, None, op0=ALU.mult), inc="V")
                if dbg == 32:
                    break
                em.wait("tensor", "V")
                mm(pA[:, 0:64], Snb, qh[:, bsl], True, False)
                mm(pA[:, 0:64], sD[0:64, :], sHm[0:64, 0:64], False, True)
                mm(pE[0:64, 256:384], kh[:, bsl], sid[:], True, True, inc="T")
                em.wait("vector", "T")
                em.op("vector", lambda e: e.tensor_copy(sKt[0:64, :], pE[0:64, 256:384]))
                em.op("vector", lambda e, h=h, blk=blk: e.tensor_copy(
                    sO[:, h * C + blk * 64:h * C + (blk + 1) * 64], pA[:, 0:64]), inc="V")
                if dbg == 33:
                    break
                em.wait("tensor", "V")
                mm(pA[:, 128:256], sKt[0:64, :], sD[0:64, :], True, True, inc="T")
                em.wait("vector", "T")
                em.op("vector", lambda e, S=Sh: e.tensor_tensor(S, S, pA[:, 128:256], ALU.subtract))
                em.op("vector", lambda e, S=Sh, Sn=Snb: e.tensor_scalar(
                    Sn, S, -1.0, None, op0=ALU.mult), inc="V")
                em.wait("tensor", "V")
                if 1 <= dbg <= 13:
                    break

            if dbg in (1,2,3,4,5,6,7,8,9,10,11,12,13,15,26,27,28,29,30,31,32,33,35,40,41,42):
                continue
            mm(pB[:, 0:C], sWg2[:, h * 128:(h + 1) * 128], sg1[:], True, True, inc="T")
            em.wait("scalar", "T")
            em.op("scalar", lambda e: e.activation(ssg[:], pB[:, 0:C], AF.Sigmoid), inc="S")
            em.wait("vector", "S")
            oh = sO[:, h * C:(h + 1) * C]
            em.op("vector", lambda e, o=oh: e.tensor_tensor(o, o, ssg[:], ALU.mult))
            em.op("vector", lambda e, o=oh: e.tensor_tensor(srg[:, 0:C], o, o, ALU.mult), inc="V")
            em.wait("tensor", "V")
            mm(pS1[0:1, 0:C], soc[:], srg[:, 0:C], True, True, inc="T")
            em.wait("scalar", "T")
            em.op("scalar", lambda e: e.activation(
                row(R_T0), pS1[0:1, 0:C], AF.Sqrt, bias=sck[0:1, 1:2], scale=1.0 / HD), inc="S")
            em.wait("vector", "S")
            em.op("vector", lambda e: e.reciprocal(row(R_T0), row(R_T0)), inc="V")
            em.wait("tensor", "V")
            mm(pBC[:, 0:C], sof[:], row(R_T0), True, True, inc="T", f32=True)
            em.wait("vector", "T")
            em.op("vector", lambda e, o=oh, h=h: e.scalar_tensor_tensor(
                o, o, sgnn[:, h:h + 1], pBC[:, 0:C], op0=ALU.mult, op1=ALU.mult), inc="V")
            em.wait("tensor", "V")

        if dbg in (14, 15, 20, 26, 27, 28, 29, 30, 31, 32, 33, 35,40,41,42):
            em.wait("sync", "V")
            em.op("sync", lambda e, s=seq, c=chs: e.dma_start(
                out=x_out[s, c].rearrange("a p t -> p a t"),
                in_=sx[:].rearrange("p (a t) -> p a t", a=DC)), inc="OD", inck=16, dma=True)
            continue
        for ot in range(0 if dbg else DC):
            buf = ot % 2
            dst = sW1[:, buf * D:(buf + 1) * D]
            sem = "W1_%d" % buf
            if first:
                em.wait("gpsimd", "T", w1_lastT[buf])
                em.wait("gpsimd", sem)
                em.op("gpsimd", lambda e, d=dst, s=wo_f[ot]: e.dma_start(
                    out=d.rearrange("p (a o) -> p a o", a=DC),
                    in_=s.rearrange("a p o -> p a o")), inc=sem + "G", inck=16)
                ldc = em.cnt[sem + "G"]
                em.wait("sync", sem + "G", ldc)
                em.op("sync", lambda e, d=wo_s[ot], s=dst: e.dma_start(
                    out=d.rearrange("p a o -> p (a o)"), in_=s), inc=sem, inck=16)
                em.wait("tensor", sem + "G", ldc)
            else:
                em.wait("sync", "T", w1_lastT[buf])
                em.wait("sync", sem)
                em.op("sync", lambda e, d=dst, s=wo_s[ot]: e.dma_start(
                    out=d, in_=s.rearrange("p a o -> p (a o)")), inc=sem, inck=16)
                em.wait("tensor", sem)
            pb = pA if ot % 2 == 0 else pB
            em.wait("tensor", "V")
            for dc in range(DC):
                mm(pb[:, 0:C], dst[:, dc * 128:(dc + 1) * 128], sO[:, dc * C:(dc + 1) * C],
                   dc == 0, dc == DC - 1, inc="T" if dc == DC - 1 else None)
            w1_lastT[buf] = em.cnt["T"]
            em.wait("vector", "T")
            em.op("vector", lambda e, ot=ot, pb=pb: e.tensor_tensor(
                sx[:, ot * C:(ot + 1) * C], sx[:, ot * C:(ot + 1) * C], pb[:, 0:C], ALU.add), inc="V")
        barrier()
        if dbg == 21:
            em.wait("sync", "V")
            em.op("sync", lambda e, s=seq, c=chs: e.dma_start(
                out=x_out[s, c].rearrange("a p t -> p a t"),
                in_=sx[:].rearrange("p (a t) -> p a t", a=DC)), inc="OD", inck=16, dma=True)
            continue

        if first and not dbg:
            for a in range(2):
                em.op("vector", lambda e, a=a: e.tensor_copy(saccb[:, 3 * a:3 * a + 1], sacc[:, a:a + 1]))
                em.op("vector", lambda e, a=a: e.tensor_tensor(
                    sacc[:, 8 + a:9 + a], sacc[:, a:a + 1], saccb[:, 3 * a:3 * a + 1], ALU.subtract))
                em.op("vector", lambda e, a=a: e.tensor_copy(saccb[:, 3 * a + 1:3 * a + 2], sacc[:, 8 + a:9 + a]))
                em.op("vector", lambda e, a=a: e.tensor_tensor(
                    sacc[:, 8 + a:9 + a], sacc[:, 8 + a:9 + a], saccb[:, 3 * a + 1:3 * a + 2], ALU.subtract))
                em.op("vector", lambda e, a=a: e.tensor_copy(saccb[:, 3 * a + 2:3 * a + 3], sacc[:, 8 + a:9 + a]),
                      inc="V" if a == 1 else None)
            em.wait("tensor", "V")
            mm(pS1[0:1, 0:1], saccb[:, 0:1], soc[:], True, False)
            mm(pS1[0:1, 0:1], saccb[:, 1:2], soc[:], False, False)
            mm(pS1[0:1, 0:1], saccb[:, 2:3], soc[:], False, True)
            mm(pS2[0:1, 0:1], saccb[:, 3:4], soc[:], True, False)
            mm(pS2[0:1, 0:1], saccb[:, 4:5], soc[:], False, False)
            mm(pS2[0:1, 0:1], saccb[:, 5:6], soc[:], False, True, inc="T")
            em.wait("vector", "T")
            em.op("vector", lambda e: e.tensor_scalar(
                srw2[:, 0:1], pS1[0:1, 0:1], 1.0 / (FO * D), 1e-5, op0=ALU.mult, op1=ALU.max))
            em.op("vector", lambda e: e.tensor_scalar(
                srw2[:, 1:2], pS2[0:1, 0:1], 1.0 / (D * FF), 1e-5, op0=ALU.mult, op1=ALU.max))
            em.op("vector", lambda e: e.tensor_scalar(
                srw2[:, 2:3], srw2[:, 0:1], 0.5, None, op0=ALU.mult))
            em.op("vector", lambda e: e.tensor_scalar(
                srw2[:, 3:4], srw2[:, 1:2], 0.5, None, op0=ALU.mult), inc="V")
            em.wait("tensor", "V")
            mm(pBC[:, 0:1], sof[:], srw2[:, 2:3], True, True)
            mm(pS1[:, 0:1], sof[:], srw2[:, 3:4], True, True, inc="T", f32=True)
            em.wait("vector", "T")
            em.op("vector", lambda e: e.tensor_copy(sthr[:, 0:1], pBC[:, 0:1]))
            em.op("vector", lambda e: e.tensor_scalar(sthr[:, 1:2], pBC[:, 0:1], -1.0, None, op0=ALU.mult))
            em.op("vector", lambda e: e.tensor_copy(sthr[:, 2:3], pS1[:, 0:1]))
            em.op("vector", lambda e: e.tensor_scalar(sthr[:, 3:4], pS1[:, 0:1], -1.0, None, op0=ALU.mult), inc="V")
            barrier()

        if dbg:
            em.wait("sync", "V")
            em.op("sync", lambda e, s=seq, c=chs: e.dma_start(
                out=x_out[s, c].rearrange("a p t -> p a t"),
                in_=sx[:].rearrange("p (a t) -> p a t", a=DC)), inc="OD", inck=16, dma=True)
            continue
        ln_stats()
        em.op("vector", lambda e: e.tensor_scalar(row(R_T0), row(R_VAR), 1e-5, None, op0=ALU.add))
        em.op("vector", lambda e: e.reciprocal(row(R_T1), row(R_T0)))
        em.op("vector", lambda e: e.tensor_tensor(row(R_T1), row(R_VAR), row(R_T1), ALU.mult))
        em.op("vector", lambda e: e.tensor_scalar(row(R_T1), row(R_T1), 1e-5, None, op0=ALU.add))
        em.op("vector", lambda e: e.tensor_tensor(row(R_T1), row(R_T0), row(R_T1), ALU.mult), inc="V")
        em.wait("scalar", "V")
        em.op("scalar", lambda e: e.activation(row(R_T1), row(R_T1), AF.Sqrt), inc="S")
        em.wait("vector", "S")
        em.op("vector", lambda e: e.reciprocal(row(R_RHO), row(R_T1)), inc="V")
        em.wait("tensor", "V")
        mm(pBC[:, 0:C], sof[:], row(R_MEAN), True, True, inc="T", f32=True)
        em.wait("vector", "T")
        em.op("vector", lambda e: e.tensor_copy(sbc[:, 0:C], pBC[:, 0:C]), inc="V")
        sA16 = sA[:].bitcast(F16)
        for g in range(DC):
            em.op("vector", lambda e, g=g, sA16=sA16: e.tensor_tensor(
                sA16[:, g * C:(g + 1) * C], sx[:, g * C:(g + 1) * C], sbc[:, 0:C], ALU.subtract),
                inc="V" if g == DC - 1 else None)
        em.op("vector", lambda e: e.memset(ssm[:, 0:C], -3e38))
        em.op("vector", lambda e: e.memset(ssm[:, C:2 * C], 3e38), inc="V")
        for g in range(DC):
            em.op("vector", lambda e, g=g, sA16=sA16: e.scalar_tensor_tensor(
                ssm[:, 0:C], sA16[:, g * C:(g + 1) * C], snw[:, g:g + 1], ssm[:, 0:C],
                op0=ALU.mult, op1=ALU.max))
            em.op("vector", lambda e, g=g, sA16=sA16: e.scalar_tensor_tensor(
                ssm[:, C:2 * C], sA16[:, g * C:(g + 1) * C], snw[:, g:g + 1], ssm[:, C:2 * C],
                op0=ALU.mult, op1=ALU.min), inc="V" if g == DC - 1 else None)
        em.op("vector", lambda e: e.scalar_tensor_tensor(
            srg[:, 0:C], ssm[:, C:2 * C], -1.0, ssm[:, 0:C], op0=ALU.mult, op1=ALU.max), inc="V")

        def token_rows(src_bf, dst_row):
            em.wait("tensor", "V")
            for t in range(TAU):
                em.op("tensor", lambda e, t=t, s=src_bf: e.transpose(
                    pGH[:, t * 128:(t + 1) * 128], s[:, t * 128:(t + 1) * 128], sid[:]),
                    inc="T" if t == TAU - 1 else None)
            em.wait("vector", "T")
            for t in range(TAU):
                em.op("vector", lambda e, t=t: e.tensor_reduce(
                    sacc[:, 8 + t:8 + t + 1], pGH[:, t * 128:(t + 1) * 128],
                    axis=AX.X, op=ALU.max), inc="V" if t == TAU - 1 else None)
            em.op("vector", lambda e: e.tensor_copy(smxb[:], sacc[:, 8:8 + TAU]), inc="V")
            em.wait("tensor", "V")
            for t in range(TAU):
                em.op("tensor", lambda e, t=t: e.transpose(
                    pGH[0:1, 512 + t * 128:512 + (t + 1) * 128], smxb[:, t:t + 1], sid[:]),
                    inc="T" if t == TAU - 1 else None)
            em.wait("vector", "T")
            em.op("vector", lambda e, d=dst_row: e.tensor_copy(d, pGH[0:1, 512:512 + C]), inc="V")

        token_rows(srg[:, 0:C], row(R_MX))
        em.op("vector", lambda e: e.tensor_tensor(row(R_MX), row(R_MX), row(R_RHO), ALU.mult))
        em.op("vector", lambda e: e.tensor_scalar(row(R_C), row(R_MX), 1e-5, None, op0=ALU.max))
        em.op("vector", lambda e: e.reciprocal(row(R_T0), row(R_C)))
        em.op("vector", lambda e: e.tensor_scalar(row(R_T0), row(R_T0), 127.0, None, op0=ALU.mult))
        em.op("vector", lambda e: e.tensor_tensor(row(R_ALPHA), row(R_RHO), row(R_T0), ALU.mult))
        em.op("vector", lambda e: e.tensor_scalar(
            row(R_C), row(R_C), srw2[:, 0:1], 1.0 / 127.0, op0=ALU.mult, op1=ALU.mult))
        em.op("vector", lambda e: e.tensor_tensor(row(R_C3), row(R_C), row(R_C), ALU.mult))
        em.op("vector", lambda e: e.tensor_tensor(row(R_C3), row(R_C3), row(R_C), ALU.mult), inc="V")
        em.wait("tensor", "V")
        mm(pBC[:, 0:C], sof[:], row(R_ALPHA), True, True, f32=True)
        mm(pS1[:, 0:C], sof[:], row(R_C3), True, True, inc="T", f32=True)
        em.wait("vector", "T")
        em.op("vector", lambda e: e.tensor_copy(sbc[:, C:2 * C], pBC[:, 0:C]))
        em.op("vector", lambda e: e.tensor_copy(sbc[:, 0:C], pS1[:, 0:C]), inc="V")
        barrier()

        if dbg == 22:
            em.wait("sync", "V")
            em.op("sync", lambda e, s=seq, c=chs: e.dma_start(
                out=x_out[s, c].rearrange("a p t -> p a t"),
                in_=sx[:].rearrange("p (a t) -> p a t", a=DC)), inc="OD", inck=16, dma=True)
            continue
        sA16q = sA[:].bitcast(F16)
        for sl in range(DC // 2):
            g0 = 2 * sl
            for j in range(2):
                em.op("vector", lambda e, g=g0 + j, j=j, sA16q=sA16q: e.scalar_tensor_tensor(
                    sy0[:, j * C:(j + 1) * C], sA16q[:, g * C:(g + 1) * C], snw[:, g:g + 1],
                    sbc[:, C:2 * C], op0=ALU.mult, op1=ALU.mult), inc="V" if j == 1 else None)
            em.wait("scalar", "V")
            em.op("scalar", lambda e: e.activation(
                sz[:, 0:2 * C], sy0[:, 0:2 * C], AF.Identity, bias=sck[:, 0:1]), inc="S")
            em.wait("vector", "S")
            em.op("vector", lambda e: e.scalar_tensor_tensor(
                sy0[:, 0:2 * C], sz[:, 0:2 * C], -1.0, sy0[:, 0:2 * C], op0=ALU.mult, op1=ALU.add))
            em.op("vector", lambda e, g0=g0: e.scalar_tensor_tensor(
                sA[:, g0 * C:(g0 + 2) * C], sy0[:, 0:2 * C], -127.5, sz[:, 0:2 * C],
                op0=ALU.is_ge, op1=ALU.add), inc="V")
        barrier()

        if dbg == 23:
            em.wait("sync", "V")
            em.op("sync", lambda e, s=seq, c=chs: e.dma_start(
                out=x_out[s, c].rearrange("a p t -> p a t"),
                in_=sx[:].rearrange("p (a t) -> p a t", a=DC)), inc="OD", inck=16, dma=True)
            continue
        if first:
            mask0 = srg[:, 0:512]
            mask1 = srg[:, 512:1024]
            for (wf, wsc, npiece, thp, thn, nslab, width, slot_w) in (
                    (wgu_f, wgu_s, 4, sthr[:, 0:1], sthr[:, 1:2], GUT, 2048, 2048),
                    (wdn_f, wdn_s, 11, sthr[:, 2:3], sthr[:, 3:4], DNT, FFC * 128, FFC * 128)):
                for sl in range(nslab):
                    tslot = sl % 2
                    tsem = "WF%d" % tslot
                    tern = sWF[:, tslot * slot_w:(tslot + 1) * slot_w]
                    em.wait("sync", "WF0")
                    em.wait("sync", "WF1")
                    em.wait("sync", "V")
                    em.wait("vector", "WF0")
                    em.wait("vector", "WF1")
                    src3 = wf[sl].rearrange("a p o -> p a o")
                    for k in range(npiece):
                        ps = k % 2
                        sem = "MW%d" % ps
                        stg = sy0[:, ps * 512:(ps + 1) * 512]
                        msk = mask0 if ps == 0 else mask1
                        if k >= 2:
                            em.wait("sync", "V", mw_state["v"][ps])
                        em.op("sync", lambda e, d=stg, s=src3, k=k: e.dma_start(
                            out=d.rearrange("p (a o) -> p a o", o=128),
                            in_=s[:, 4 * k:4 * (k + 1), :]), inc=sem, inck=16, dma=True)
                        em.wait("vector", sem)
                        em.op("vector", lambda e, stg=stg, msk=msk, thn=thn: e.tensor_scalar(
                            msk, stg, thn, None, op0=ALU.is_lt))
                        em.op("vector", lambda e, t=tern, k=k, stg=stg, msk=msk, thp=thp:
                              e.scalar_tensor_tensor(
                                  t[:, k * 512:(k + 1) * 512], stg, thp, msk,
                                  op0=ALU.is_gt, op1=ALU.subtract), inc="V")
                        mw_state["v"][ps] = em.cnt["V"]
                    em.wait("sync", "V")
                    em.op("sync", lambda e, d=wsc[sl], s=tern: e.dma_start(
                        out=d.rearrange("p a o -> p (a o)"), in_=s), inc=tsem, inck=16, dma=True)

        em.op("vector", lambda e: e.memset(srg[:, 2 * C:3 * C], 0.0), inc="V")
        for j in range(FFC):
            slabs = []
            for which, oti in ((0, j), (1, j + FFC)):
                buf = 2 * ((j % 2)) + which
                sem = "WF%d" % buf
                raw = sWF[:, buf * 2048:(buf + 1) * 2048]
                em.wait("sync", "T", wf_lastT[buf])
                em.wait("sync", sem)
                em.op("sync", lambda e, d=raw, s=wgu_s[oti]: e.dma_start(
                    out=d, in_=s.rearrange("p a o -> p (a o)")), inc=sem, inck=16)
                slabs.append((buf, raw, oti))
            em.wait("tensor", "WF%d" % slabs[0][0])
            em.wait("tensor", "WF%d" % slabs[1][0])
            if first:
                for bi, (buf, raw, oti) in enumerate(slabs):
                    pcs = pS1 if bi == 0 else pS2
                    for dc in range(DC):
                        mm(pcs[:, 0:1], raw[:, dc * 128:(dc + 1) * 128], soc[:],
                           dc == 0, dc == DC - 1, inc="T" if dc == DC - 1 else None)
                em.wait("vector", "T")
                for bi, (buf, raw, oti) in enumerate(slabs):
                    pcs = pS1 if bi == 0 else pS2
                    em.op("vector", lambda e, oti=oti, pcs=pcs: e.tensor_scalar(
                        scol[:, oti:oti + 1], pcs[:, 0:1], -128.0, None, op0=ALU.mult),
                        inc="V" if bi == 1 else None)
                em.wait("tensor", "V")
            gslab, uslab = slabs[0][1], slabs[1][1]
            em.wait("tensor", "S")
            em.wait("tensor", "V")
            for dc in range(DC):
                mm(pA[:, 0:C], gslab[:, dc * 128:(dc + 1) * 128], sA[:, dc * C:(dc + 1) * C],
                   dc == 0, dc == DC - 1)
            for dc in range(DC):
                mm(pB[:, 0:C], uslab[:, dc * 128:(dc + 1) * 128], sA[:, dc * C:(dc + 1) * C],
                   dc == 0, dc == DC - 1, inc="T" if dc == DC - 1 else None)
            wf_lastT[slabs[0][0]] = em.cnt["T"]
            wf_lastT[slabs[1][0]] = em.cnt["T"]
            em.wait("scalar", "T")
            em.op("scalar", lambda e, j=j: e.activation(
                srg[:, 0:C], pA[:, 0:C], AF.Relu, bias=scol[:, j:j + 1]))
            em.op("scalar", lambda e: e.activation(srg[:, C:2 * C], srg[:, 0:C], AF.Square), inc="S")
            em.wait("vector", "S")
            sB16 = sB[:].bitcast(F16)
            em.op("vector", lambda e, j=j: e.scalar_tensor_tensor(
                sy0[:, 0:C], pB[:, 0:C], scol[:, FFC + j:FFC + j + 1],
                srg[:, C:2 * C], op0=ALU.add, op1=ALU.mult))
            em.op("vector", lambda e, j=j, sB16=sB16: e.tensor_tensor(
                sB16[:, j * C:(j + 1) * C], sy0[:, 0:C], sbc[:, 0:C], ALU.mult))
            em.op("vector", lambda e, j=j, sB16=sB16: e.tensor_tensor(
                srg[:, 0:C], sB16[:, j * C:(j + 1) * C], sB16[:, j * C:(j + 1) * C], ALU.mult))
            em.op("vector", lambda e: e.tensor_tensor(
                srg[:, 2 * C:3 * C], srg[:, 2 * C:3 * C], srg[:, 0:C], ALU.max), inc="V")
        barrier()

        if dbg == 24:
            em.wait("sync", "V")
            em.op("sync", lambda e, s=seq, c=chs: e.dma_start(
                out=x_out[s, c].rearrange("a p t -> p a t"),
                in_=sx[:].rearrange("p (a t) -> p a t", a=DC)), inc="OD", inck=16, dma=True)
            continue
        em.op("vector", lambda e: e.tensor_copy(srg[:, 0:C], srg[:, 2 * C:3 * C]), inc="V")
        token_rows(srg[:, 0:C], row(R_MX))
        em.wait("scalar", "V")
        em.op("scalar", lambda e: e.activation(row(R_MX), row(R_MX), AF.Sqrt), inc="S")
        em.wait("vector", "S")
        em.op("vector", lambda e: e.tensor_scalar(row(R_T0), row(R_MX), 1e-5, None, op0=ALU.max))
        em.op("vector", lambda e: e.reciprocal(row(R_S2), row(R_T0)))
        em.op("vector", lambda e: e.tensor_scalar(row(R_S2), row(R_S2), 127.0, None, op0=ALU.mult))
        em.op("vector", lambda e: e.tensor_scalar(
            row(R_OS2), row(R_T0), srw2[:, 1:2], 1.0 / 127.0, op0=ALU.mult, op1=ALU.mult), inc="V")
        em.wait("tensor", "V")
        mm(pBC[:, 0:C], sof[:], row(R_S2), True, True, f32=True)
        mm(pS1[:, 0:C], sof[:], row(R_OS2), True, True, inc="T", f32=True)
        em.wait("vector", "T")
        em.op("vector", lambda e: e.tensor_copy(sbc[:, 0:C], pBC[:, 0:C]))
        em.op("vector", lambda e: e.tensor_copy(sbc[:, C:2 * C], pS1[:, 0:C]), inc="V")
        barrier()

        sB16q = sB[:].bitcast(F16)
        for sl in range(FFC // 2):
            f0 = 2 * sl
            for j in range(2):
                em.op("vector", lambda e, f=f0 + j, j=j, sB16q=sB16q: e.tensor_tensor(
                    sy0[:, j * C:(j + 1) * C], sB16q[:, f * C:(f + 1) * C], sbc[:, 0:C], ALU.mult),
                    inc="V" if j == 1 else None)
            em.wait("scalar", "V")
            em.op("scalar", lambda e: e.activation(
                sz[:, 0:2 * C], sy0[:, 0:2 * C], AF.Identity, bias=sck[:, 0:1]), inc="S")
            em.wait("vector", "S")
            em.op("vector", lambda e: e.scalar_tensor_tensor(
                sy0[:, 0:2 * C], sz[:, 0:2 * C], -1.0, sy0[:, 0:2 * C], op0=ALU.mult, op1=ALU.add))
            em.op("vector", lambda e, f0=f0: e.scalar_tensor_tensor(
                sB[:, f0 * C:(f0 + 2) * C], sy0[:, 0:2 * C], -127.5, sz[:, 0:2 * C],
                op0=ALU.is_ge, op1=ALU.add), inc="V")
        barrier()

        if dbg == 25:
            em.wait("sync", "V")
            em.op("sync", lambda e, s=seq, c=chs: e.dma_start(
                out=x_out[s, c].rearrange("a p t -> p a t"),
                in_=sx[:].rearrange("p (a t) -> p a t", a=DC)), inc="OD", inck=16, dma=True)
            continue
        for ot in range(DNT):
            buf = ot % 2
            raw = sWF[:, buf * FFC * 128:(buf + 1) * FFC * 128]
            sems_cover = ["WF%d" % k for k in range(4)] if ot < 2 else ["WF%d" % buf]
            em.wait("sync", "T", wf_lastT[buf])
            for sm in sems_cover:
                em.wait("sync", sm)
            em.op("sync", lambda e, d=raw, s=wdn_s[ot]: e.dma_start(
                out=d, in_=s.rearrange("p a o -> p (a o)")), inc="WF%d" % buf, inck=16)
            em.wait("tensor", "WF%d" % buf)
            if first:
                for fc in range(FFC):
                    mm(pS1[:, 0:1], raw[:, fc * 128:(fc + 1) * 128], soc[:],
                       fc == 0, fc == FFC - 1, inc="T" if fc == FFC - 1 else None)
                em.wait("vector", "T")
                em.op("vector", lambda e, ot=ot: e.tensor_scalar(
                    scol[:, GUT + ot:GUT + ot + 1], pS1[:, 0:1], -128.0, None, op0=ALU.mult), inc="V")
                em.wait("tensor", "V")
            pb = pA if ot % 2 == 0 else pB
            em.wait("tensor", "V")
            for fc in range(FFC):
                mm(pb[:, 0:C], raw[:, fc * 128:(fc + 1) * 128], sB[:, fc * C:(fc + 1) * C],
                   fc == 0, fc == FFC - 1, inc="T" if fc == FFC - 1 else None)
            wf_lastT[buf] = em.cnt["T"]
            em.wait("vector", "T")
            em.op("vector", lambda e, ot=ot, pb=pb: e.scalar_tensor_tensor(
                srg[:, 0:C], pb[:, 0:C], scol[:, GUT + ot:GUT + ot + 1],
                sbc[:, C:2 * C], op0=ALU.add, op1=ALU.mult))
            em.op("vector", lambda e, ot=ot: e.tensor_tensor(
                sx[:, ot * C:(ot + 1) * C], sx[:, ot * C:(ot + 1) * C], srg[:, 0:C], ALU.add), inc="V")
        barrier()

        em.op("sync", lambda e, s=seq, c=chs: e.dma_start(
            out=x_out[s, c].rearrange("a p t -> p a t"),
            in_=sx[:].rearrange("p (a t) -> p a t", a=DC)), inc="OD", inck=16)

    for nm in ("OD", "XL", "CO", "W1_0", "W1_1", "WF0", "WF1", "WF2", "WF3"):
        em.wait("sync", nm)
    for nm in ("CR", "MW0", "MW1", "W1_0G", "W1_1G", "WF0G", "WF1G", "WF2G", "WF3G"):
        em.wait("gpsimd", nm)

    with ctx:
        with nc.Block() as block:
            @block.sync
            def _(e):
                for fn in em.q["sync"]:
                    fn(e)

            @block.tensor
            def _(e):
                for fn in em.q["tensor"]:
                    fn(e)

            @block.vector
            def _(e):
                for fn in em.q["vector"]:
                    fn(e)

            @block.scalar
            def _(e):
                for fn in em.q["scalar"]:
                    fn(e)

            @block.gpsimd
            def _(e):
                for fn in em.q["gpsimd"]:
                    fn(e)

    return nc


def _prep(inputs):
    x = np.asarray(inputs["L0_attn_x"], np.float32)
    Wq = np.asarray(inputs["L0_attn_Wq"], np.float32)
    Wk = np.asarray(inputs["L0_attn_Wk"], np.float32)
    Wv = np.asarray(inputs["L0_attn_Wv"], np.float32)
    Wo = np.asarray(inputs["L0_attn_Wo"], np.float32)
    Wbeta = np.asarray(inputs["L0_attn_Wbeta"], np.float32)
    Wg1 = np.asarray(inputs["L0_attn_Wg1"], np.float32)
    Wg2 = np.asarray(inputs["L0_attn_Wg2"], np.float32)
    gnorm = np.asarray(inputs["L0_attn_g_norm"], np.float32)
    wgu = np.asarray(inputs["L0_ffn_w_gate_up"], np.float32)
    wdn = np.asarray(inputs["L0_ffn_w_down"], np.float32)
    nw = np.asarray(inputs["L0_ffn_norm_weight"], np.float32)

    def wT(W):
        return np.ascontiguousarray(
            W.T.reshape(DC, 128, W.shape[0] // 128, 128).transpose(2, 0, 1, 3))

    shared = dict(
        wq_f=wT(Wq), wk_f=wT(Wk), wv_f=wT(Wv), wo_f=wT(Wo), wgu_f=wT(wgu),
        wdn_f=np.ascontiguousarray(wdn.T.reshape(FFC, 128, DNT, 128).transpose(2, 0, 1, 3)),
        wbeta_f=np.ascontiguousarray(Wbeta.T.reshape(DC, 128, H).transpose(1, 0, 2)),
        wg1_f=np.ascontiguousarray(Wg1.T.reshape(DC, 128, 128)),
        wg2_f=np.ascontiguousarray(Wg2.T),
        gnorm_f=np.ascontiguousarray(gnorm.reshape(DC, 128).T),
        normw_f=np.ascontiguousarray(nw.reshape(DC, 128).T),
        c_id=np.eye(128, dtype=np.float32).astype(BF),
        c_nid=(-np.eye(128, dtype=np.float32)).astype(BF),
        c_mu=np.triu(np.ones((128, 128), np.float32), 1).astype(BF),
        c_mud=np.triu(np.ones((128, 128), np.float32), 0).astype(BF),
        c_ones1=np.ones((1, 128), np.float32).astype(BF),
        c_onescol=np.ones((128, 1), np.float32).astype(BF),
        c_idf=np.eye(128, dtype=np.float32),
    )
    in_maps = []
    for c in range(NCORES):
        xc = x[SEQS * c:SEQS * (c + 1)]
        x_fm = np.ascontiguousarray(
            xc.transpose(0, 2, 1).reshape(SEQS, DC, 128, N // C, C).transpose(0, 3, 1, 2, 4))
        m = dict(shared)
        m["x_in"] = x_fm
        in_maps.append(m)
    return in_maps


def kernel(**inputs):
    in_maps = _prep(inputs)
    nc = build_nc()
    res = run_bass_kernel_spmd(nc, in_maps, core_ids=list(range(NCORES)))
    out = np.empty((NCORES * SEQS, N, D), np.float32)
    for c in range(NCORES):
        o = np.asarray(res.results[c]["x_out"])
        o = o.transpose(0, 2, 3, 1, 4).reshape(SEQS, D, N).transpose(0, 2, 1)
        out[SEQS * c:SEQS * (c + 1)] = o
    return out
